# version 13
# baseline (speedup 1.0000x reference)
import math
import numpy as np
import concourse.bass as bass
import concourse.mybir as mybir
from concourse.alu_op_type import AluOpType as alu

F32 = mybir.dt.float32
BF16 = mybir.dt.bfloat16
_PSUM_KEYS = frozenset(["pbA", "pbB", "pbC", "pbig", "pbig2", "ptr"])
AF = mybir.ActivationFunctionType
NCORES = 8
H, HD, CH, DM, IN0, IN1, NS, RK = 16, 128, 128, 2048, 8192, 8192, 16, 64


class Sem:
    def __init__(self, name, sem, step, oneshot=False):
        self.name, self.sem, self.step, self.oneshot, self.n = name, sem, step, oneshot, 0
        self.owner = None


class Lane:
    def __init__(self, name, eng, sem):
        self.name, self.eng, self.s, self.waited = name, eng, sem, {}


class KB:
    def __init__(self, nc):
        self.nc = nc
        self.lanes, self.sems = {}, {}
        for nm, eng in (("PE", nc.tensor), ("DVE", nc.vector), ("ACT", nc.scalar),
                        ("POOL", nc.gpsimd)):
            self.lanes[nm] = Lane(nm, eng, self._sem(nm, 1))
        for nm, eng in (("SPD", nc.sync), ("POOLD", nc.gpsimd)):
            self.lanes[nm] = Lane(nm, eng, None)
        self.reg = {}
        self._consts = {}
        self.nops = 0

    def _sem(self, name, step):
        h = self.nc.ctx.enter_context(self.nc.semaphore("s_" + name))
        s = Sem(name, h, step)
        self.sems[name] = s
        return s

    def slot(self, name, oneshot=False):
        s = self._sem(name, 16)
        s.oneshot = oneshot
        return s

    def emit(self, lane_name, reads, writes, fn, slot=None):
        L = self.lanes[lane_name]
        if self._consts and slot is None and "cpool" not in reads:
            reads = list(reads) + ["cpool"]
        s = slot if slot is not None else L.s
        assert s is not None, f"lane {lane_name} needs a slot"
        if s.step == 16:
            assert s.owner in (None, lane_name), f"slot {s.name}: {lane_name} vs {s.owner}"
            s.owner = lane_name
        toks = []
        for r in reads:
            e = self.reg.get(r)
            if e and e["w"]:
                toks.append(e["w"])
            if e and r in _PSUM_KEYS:
                toks.extend(t for t in e["r"] if t[0] is not s)
        for w in writes:
            e = self.reg.get(w)
            if e:
                if e["w"]:
                    toks.append(e["w"])
                toks.extend(e["r"])
        need = {}
        for (ss, v) in toks:
            if v > need.get(ss.name, 0):
                need[ss.name] = v
        for nm, v in sorted(need.items()):
            ss = self.sems[nm]
            if ss.step == 16 and v != ss.n:
                if ss.oneshot:
                    v = ss.n
                else:
                    raise AssertionError(f"stale DMA wait {nm}: {v} != {ss.n}")
            if L.waited.get(nm, -1) >= v:
                continue
            L.eng.wait_ge(ss.sem, v)
            L.waited[nm] = v
        inst = fn(L.eng)
        inst.then_inc(s.sem, s.step)
        s.n += s.step
        self.nops += 1
        tok = (s, s.n)
        for w in writes:
            self.reg[w] = {"w": tok, "r": []}
        for r in reads:
            self.reg.setdefault(r, {"w": None, "r": []})["r"].append(tok)
        return tok

    def barrier(self):
        for a in ("PE", "DVE", "ACT", "POOL", "SPD"):
            La = self.lanes[a]
            for s in self.sems.values():
                if s.n > 0 and La.waited.get(s.name, -1) < s.n:
                    La.eng.wait_ge(s.sem, s.n)
                    La.waited[s.name] = s.n

    def init_consts(self, values, alloc=None):
        values = [float(v) for v in values]
        alloc = alloc or self.nc.alloc_sbuf_tensor
        cp = alloc("cpool", [128, len(values)], F32)
        self.cpool = cp
        def fill(e):
            last = None
            for i, v in enumerate(values):
                last = e.memset(cp[:, i:i + 1], v)
                self.nc.const_aps.register(F32, v, cp[:, i:i + 1])
                self._consts[v] = cp[:, i:i + 1]
            return last
        self.emit("DVE", [], ["cpool"], fill)


def _hadamard(n):
    h = np.array([[1.0]])
    while h.shape[0] < n:
        h = np.block([[h, h], [h, -h]])
    return h


def build_nc(T=1024, upto="full"):
    assert T % CH == 0
    NCH = T // CH
    TS = min(512, T)
    NT = T // TS
    TH = T // 2 if T >= 256 else T
    NHALF = T // TH
    TSH = min(TS, TH)
    NTH = TH // TSH
    nc = bass.Bass("TRN2", target_bir_lowering=False)
    kb = KB(nc)
    E = kb.emit
    DT = nc.dram_tensor

    dx = DT("xT", [DM, T], F32, kind="ExternalInput")
    dy = DT("yT", [DM, T], F32, kind="ExternalOutput")
    dwqkv = DT("wqkvT", [DM, 3 * DM], F32, kind="ExternalInput")
    dwout = DT("woutT", [DM, DM], F32, kind="ExternalInput")
    dbqkv = DT("bqkv", [128, 48], F32, kind="ExternalInput")
    dbout = DT("bout", [128, 16], F32, kind="ExternalInput")
    dwg = DT("wg_p", [DM, IN0], F32, kind="ExternalInput")
    dwu = DT("wu_p", [DM, IN0], F32, kind="ExternalInput")
    dwd = DT("wd_p", [IN0, DM], F32, kind="ExternalInput")
    dwdtin = DT("wdtinT", [DM, RK], F32, kind="ExternalInput")
    dwdtout = DT("wdtoutT", [RK, DM], F32, kind="ExternalInput")
    dwb = DT("wbT", [DM, NS], F32, kind="ExternalInput")
    dwc = DT("wcT", [DM, NS], F32, kind="ExternalInput")
    dalog = DT("alog_l", [128, 16 * NS], F32, kind="ExternalInput")
    dbdt = DT("bdt", [128, 16], F32, kind="ExternalInput")
    ddvec = DT("dvec", [128, 16], F32, kind="ExternalInput")
    dsuug = DT("su_ug", [128, 16], F32, kind="ExternalInput")
    dsudp = DT("su_dp", [128, 64], F32, kind="ExternalInput")
    dwscu = DT("wscu", [128, 64], F32, kind="ExternalInput")
    dwscg = DT("wscg", [128, 64], F32, kind="ExternalInput")
    dwscdp = DT("wscdp", [128, 16], F32, kind="ExternalInput")
    dw1u = DT("w1upT", [DM, IN1], F32, kind="ExternalInput")
    dw1g = DT("w1gateT", [DM, IN1], F32, kind="ExternalInput")
    dw1d = DT("w1downT", [IN1, DM], F32, kind="ExternalInput")
    dw1ub = DT("w1ub", [DM, IN1], BF16)
    dw1gb = DT("w1gb", [DM, IN1], BF16)
    dw1db = DT("w1db", [IN1, DM], BF16)
    dnslp = DT("nslopes_b", [128, 16], F32, kind="ExternalInput")
    dp1 = DT("p1b", [128, 128], F32, kind="ExternalInput")
    ddk = DT("dkb", [128, 128], F32, kind="ExternalInput")
    ddiff = DT("diffc", [128, 128], F32, kind="ExternalInput")
    dcaus = DT("causal", [128, 128], F32, kind="ExternalInput")
    dh128 = DT("h128", [128, 128], F32, kind="ExternalInput")
    di128 = DT("i128", [128, 128], F32, kind="ExternalInput")
    dct = DT("ct_l0", [128, 128], F32, kind="ExternalInput")
    dmask8 = DT("mask8", [128, 8], F32, kind="ExternalInput")
    dwloc = DT("wloc0", [H * HD, HD], F32)
    dwall = DT("wall0", [NCORES * H * HD, HD], F32)
    dsloc = DT("sloc1", [DM, NS + 1], F32)
    dbc = DT("dbc", [2 * NS, 1024], BF16)
    dsall = DT("sall1", [NCORES * DM, NS + 1], F32)

    _pc = [0]

    def A(name, shape, dtype):
        sz = int(np.prod(shape[1:])) * (2 if dtype == BF16 else (1 if dtype == mybir.dt.uint8 else 4))
        _pc[0] = (_pc[0] + 31) // 32 * 32
        t = nc.alloc_sbuf_tensor(name, shape, dtype)
        _pc[0] += sz
        return t
    x_sb = A("x_sb", [128, 16, T], F32)
    SB_BASE = None
    for _al in nc.m.functions[0].allocations:
        if isinstance(_al, mybir.MemoryLocationSet) and _al.memorylocations and \
                _al.memorylocations[0].name == "x_sb":
            SB_BASE = _al.memorylocations[0].addr
    assert SB_BASE is not None
    xnb_off = SB_BASE + (_pc[0] + 31) // 32 * 32
    xnb = A("xnb", [128, 16, T], BF16)
    scrA = A("scrA", [128, TS], F32)
    scrB = A("scrB", [128, TS], F32)
    scrC = A("scrC", [128, T], BF16)
    p1s = A("p1s_t", [128, 128], F32)
    nslp = A("nslp_t", [128, 16], F32)
    dkbt = A("dkb_t", [128, 128], F32)
    difft = A("diff_t", [128, 128], F32)
    caust = A("caus_t", [128, 128], F32)
    h128t = A("h128_t", [128, 128], BF16)
    i128t = A("i128_t", [128, 128], BF16)
    ctt = A("ct_t", [128, 128], F32)
    mask8t = A("mask8_t", [128, 8], F32)
    onesc = A("onesc", [128, 1], BF16)
    onesr = A("onesr", [1, 128], BF16)
    onesrF = A("onesrF", [1, 128], F32)
    bqkvt = A("bqkv_t", [128, 48], F32)
    boutt = A("bout_t", [128, 16], F32)
    asbt = A("a_t", [128, 16 * NS], F32)
    bdtt = A("bdt_t", [128, 16], F32)
    dvect = A("dvec_t", [128, 16], F32)
    suugt = A("suug_t", [128, 16], F32)
    sudpt = A("sudp_t", [128, 64], F32)
    wscut = A("wscu_t", [128, 64], F32)
    wscgt = A("wscg_t", [128, 64], F32)
    wscdpt = A("wscdp_t", [128, 16], F32)
    lamct = A("lamc_t", [128, 16], F32)
    wst = A("wst", [128, 16, NS + 1], F32)
    est = A("est", [128, 16 * NS], F32)
    stats_off = (_pc[0] + 31) // 32 * 32
    stats = A("stats", [33, T], F32)
    stMR_off = (_pc[0] + 31) // 32 * 32
    stM = A("stM", [1, T], BF16)
    stR = A("stR", [1, T], BF16)
    rsq = A("rsq", [128, 2], F32)
    kb.init_consts([1.0, -1.0, 0.0, 0.5, 1.5, 2048.0, 1e-5], alloc=A)

    AR_BASE = SB_BASE + (_pc[0] + 31) // 32 * 32
    AR_SIZE = (SB_BASE + 212863 - 64) - AR_BASE
    _cur = [0]

    def ar_reset():
        _cur[0] = 0

    def ar(name, shape, dtype, at=None):
        nbytes = int(np.prod(shape[1:])) * (2 if dtype == BF16 else 4)
        if at is None:
            off = _cur[0]
            _cur[0] = (off + nbytes + 31) // 32 * 32
            assert _cur[0] <= AR_SIZE, f"arena overflow: {name} needs {_cur[0]} > {AR_SIZE}"
            return nc.alloc_sbuf_tensor_at(name, shape, dtype, offset=AR_BASE + off)
        return nc.alloc_sbuf_tensor_at(name, shape, dtype, offset=at)

    P = nc.alloc_psum_tensor
    pbA = P("pbA", [128, 512], F32)
    pbB = P("pbB", [128, 512], F32)
    pbC = P("pbC", [128, 512], F32)
    pbig = P("pbig", [128, 1024], F32)
    ptr = P("ptr", [128, 1024], BF16)
    pbig2 = P("pbig2", [128, 1024], F32)

    sl_tab = kb.slot("tab", oneshot=True)
    sl_tab2 = kb.slot("tab2", oneshot=True)
    sl_x = kb.slot("xl", oneshot=True)
    sl_out = kb.slot("outp", oneshot=True)
    sl_cb = kb.slot("cb")
    sl_cg = kb.slot("cg")

    for dst, src in ((nslp, dnslp), (p1s, dp1), (dkbt, ddk), (difft, ddiff),
                     (caust, dcaus), (ctt, dct), (mask8t, dmask8), (bqkvt, dbqkv),
                     (boutt, dbout), (bdtt, dbdt), (dvect, ddvec), (suugt, dsuug),
                     (sudpt, dsudp), (wscut, dwscu), (wscgt, dwscg), (wscdpt, dwscdp)):
        E("SPD", [], [dst.name], lambda e, d=dst, s=src: e.dma_start(out=d.ap(), in_=s.ap()),
          slot=sl_tab)
    E("SPD", [], ["a_t"], lambda e: e.dma_start(out=asbt.ap(), in_=dalog.ap()), slot=sl_tab)
    for dst, src in ((h128t, dh128), (i128t, di128)):
        E("POOLD", [], [dst.name], lambda e, d=dst, s=src: e.dma_start(out=d.ap(), in_=s.ap()),
          slot=sl_tab2)
    E("DVE", [], ["ones"], lambda e: (e.memset(onesc.ap(), 1.0), e.memset(onesr.ap(), 1.0),
                                      e.memset(onesrF.ap(), 1.0))[-1])
    E("ACT", ["a_t"], ["a_t"], lambda e: e.activation(asbt.ap(), asbt.ap(), AF.Exp))
    E("DVE", ["a_t"], ["a_t"], lambda e: e.tensor_scalar_mul(asbt.ap(), asbt.ap(), -1.0))
    E("ACT", ["nslp_t"], ["lamc_t"], lambda e: e.activation(
        lamct.ap(), nslp.ap(), AF.Exp, bias=0.0, scale=128.0))
    E("ACT", [], ["rsq"], lambda e: e.activation(rsq[:, 1:2], kb._consts[2048.0], AF.Sqrt))
    E("DVE", ["rsq"], ["rsq"], lambda e: e.reciprocal(rsq[:, 1:2], rsq[:, 1:2]))
    E("ACT", ["rsq"], ["rsq"], lambda e: e.activation(rsq[:, 0:1], kb._consts[0.5], AF.Sqrt))
    E("DVE", ["rsq", "wscu_t"], ["wscu_t"], lambda e: e.tensor_scalar(
        wscut.ap(), wscut.ap(), rsq[:, 1:2], None, alu.mult))
    E("DVE", ["rsq", "wscg_t"], ["wscg_t"], lambda e: e.tensor_scalar(
        wscgt.ap(), wscgt.ap(), rsq[:, 1:2], None, alu.mult))
    E("DVE", ["rsq", "wscdp_t"], ["wscdp_t"], lambda e: e.tensor_scalar(
        wscdpt.ap(), wscdpt.ap(), rsq[:, 0:1], 1.0 / 64.0, alu.mult, alu.mult))
    E("SPD", [], ["x_load"], lambda e: e.dma_start(
        out=x_sb.ap(), in_=dx.rearrange("(c p) t -> p c t", p=128)), slot=sl_x)
    for c in range(16):
        kb.reg["x%d" % c] = dict(kb.reg["x_load"])
    sl_pcg = kb.slot("pcg", oneshot=True)
    sl_pcu = kb.slot("pcu", oneshot=True)
    sl_pcd = kb.slot("pcd", oneshot=True)

    def preconv(i, anchor):
        E("POOLD", [anchor], ["dw1gb"], lambda e, i=i: e.dma_start(
            out=dw1gb[i * 256:(i + 1) * 256, :], in_=dw1g[i * 256:(i + 1) * 256, :]),
          slot=sl_pcg)
        E("POOLD", [anchor], ["dw1ub"], lambda e, i=i: e.dma_start(
            out=dw1ub[i * 256:(i + 1) * 256, :], in_=dw1u[i * 256:(i + 1) * 256, :]),
          slot=sl_pcu)
        E("POOLD", [anchor], ["dw1db"], lambda e, i=i: e.dma_start(
            out=dw1db[i * 1024:(i + 1) * 1024, :], in_=dw1d[i * 1024:(i + 1) * 1024, :]),
          slot=sl_pcd)

    def layernorm():
        for c in range(16):
            E("DVE", ["x%d" % c], ["scrC"], lambda e, c=c: e.tensor_copy(scrC[:, 0:T], x_sb[:, c, :]))
            for t in range(NT):
                E("PE", ["scrC", "ones"], ["pbig"], lambda e, c=c, t=t: e.matmul(
                    pbig[0:1, t * TS:(t + 1) * TS], onesc.ap(), scrC[:, t * TS:(t + 1) * TS],
                    start=(c == 0), stop=(c == 15)))
        E("DVE", ["pbig"], ["st0"], lambda e: e.tensor_scalar_mul(stats[32:33, 0:T], pbig[0:1, 0:T], 1.0 / DM))
        for c in range(16):
            E("ACT", ["x%d" % c], ["scrC"], lambda e, c=c: e.activation(
                scrC[:, 0:T], x_sb[:, c, :], AF.Square))
            for t in range(NT):
                E("PE", ["scrC", "ones", "st0"], ["pbig"], lambda e, c=c, t=t: e.matmul(
                    pbig[0:1, t * TS:(t + 1) * TS], onesc.ap(), scrC[:, t * TS:(t + 1) * TS],
                    start=(c == 0), stop=(c == 15)))
        E("DVE", ["pbig", "st0"], ["st1"], lambda e: e.tensor_scalar_mul(stats[0:1, 0:T], pbig[0:1, 0:T], 1.0 / DM))
        E("ACT", ["st0"], ["stM"], lambda e: e.activation(stM[0:1, 0:T], stats[32:33, 0:T], AF.Square))
        E("DVE", ["st1", "stM"], ["st1"], lambda e: e.tensor_tensor(
            stats[0:1, 0:T], stats[0:1, 0:T], stM[0:1, 0:T], alu.subtract))
        E("ACT", ["st1"], ["st1"], lambda e: e.activation(stats[0:1, 0:T], stats[0:1, 0:T],
                                                          AF.Sqrt, bias=1e-5))
        E("DVE", ["st1"], ["st1"], lambda e: e.reciprocal(stats[0:1, 0:T], stats[0:1, 0:T]))
        E("DVE", ["st1"], ["stR"], lambda e: e.tensor_copy(stR[0:1, 0:T], stats[0:1, 0:T]))
        E("ACT", ["st0"], ["stM"], lambda e: e.activation(stM[0:1, 0:T], stats[32:33, 0:T], AF.Copy))
        E("DVE", ["st1", "stM"], ["stM"], lambda e: e.tensor_tensor(
            stM[0:1, 0:T], stM[0:1, 0:T], stats[0:1, 0:T], alu.mult))
        for t in range(NT):
            E("PE", ["stR", "ones"], ["pbA"], lambda e, t=t: e.matmul(
                pbA[:, 0:TS], onesr.ap(), stR[0:1, t * TS:(t + 1) * TS], start=True, stop=True))
            E("PE", ["stM", "ones"], ["pbB"], lambda e, t=t: e.matmul(
                pbB[:, 0:TS], onesr.ap(), stM[0:1, t * TS:(t + 1) * TS], start=True, stop=True))
            for c in range(16):
                E("DVE", ["x%d" % c, "pbA"], ["scrA"], lambda e, c=c, t=t: e.tensor_tensor(
                    scrA[:, 0:TS], x_sb[:, c, t * TS:(t + 1) * TS], pbA[:, 0:TS], alu.mult))
                E("DVE", ["scrA", "pbB"], ["xnb%d" % c], lambda e, c=c, t=t: e.tensor_tensor(
                    xnb[:, c, t * TS:(t + 1) * TS], scrA[:, 0:TS], pbB[:, 0:TS], alu.subtract))

    layernorm()
    if upto == "ln1":
        _finish(kb, E, x_sb, dy, sl_out, T)
        return nc, kb

    ar_reset()
    spark = ar("spark", [128, H, 128], F32)
    kpark = ar("kpark", [128, H, T], BF16)
    vpark = ar("vpark", [128, H, T], BF16)
    wk_off = _cur[0]
    wkbuf = ar("wkbuf", [128, 16, 128], BF16)
    wvbuf = ar("wvbuf", [128, 16, 128], BF16)
    wqbuf = ar("wqbuf", [128, 16, 128], BF16)
    dkh = ar("dkh", [128, 128], BF16)
    dmh = ar("dmh", [128, 128], BF16)
    dqh = ar("dqh", [128, 128], BF16)
    qth = ar("qth", [128, T], BF16)
    qpc = ar("qpc", [128, 128], BF16)
    ktk = [ar("ktk%d" % i, [128, 128], BF16) for i in range(2)]
    kdec = [ar("kdec%d" % i, [128, 128], BF16) for i in range(2)]
    msc = [ar("msc%d" % i, [128, 128], BF16) for i in range(2)]
    sbf = ar("sbf", [128, 128], BF16)
    walk = ar("walk", [128, H, 128], F32, at=AR_BASE + wk_off)
    sl_wk = kb.slot("wk")
    sl_wv = kb.slot("wv")
    sl_wq = kb.slot("wq")
    sl_wo = kb.slot("wo")

    def head_w(e, dst, col):
        return e.dma_start(out=dst.ap(), in_=dwqkv[:, col:col + HD]
                           .rearrange("(kc p) f -> p kc f", p=128))

    for h in range(H):
        E("POOLD", [], ["wkbuf"], lambda e, h=h: head_w(e, wkbuf, DM + h * HD), slot=sl_wk)
        E("POOLD", [], ["wvbuf"], lambda e, h=h: head_w(e, wvbuf, 2 * DM + h * HD), slot=sl_wv)
        for t in range(NT):
            E("PE", ["wkbuf"] + ["xnb%d" % c for c in range(16)], ["pbig"],
              lambda e, t=t: [e.matmul(pbig[:, t * TS:(t + 1) * TS], wkbuf[:, kc, :],
                                       xnb[:, kc, t * TS:(t + 1) * TS],
                                       start=(kc == 0), stop=(kc == 15)) for kc in range(16)][-1])
            E("ACT", ["pbig", "bqkv_t"], ["kp%d" % h], lambda e, h=h, t=t: e.activation(
                kpark[:, h, t * TS:(t + 1) * TS], pbig[:, t * TS:(t + 1) * TS], AF.Identity,
                bias=bqkvt[:, 16 + h:17 + h]))
        for t in range(NT):
            E("PE", ["wvbuf"] + ["xnb%d" % c for c in range(16)], ["pbig"],
              lambda e, t=t: [e.matmul(pbig[:, t * TS:(t + 1) * TS], wvbuf[:, kc, :],
                                       xnb[:, kc, t * TS:(t + 1) * TS],
                                       start=(kc == 0), stop=(kc == 15)) for kc in range(16)][-1])
            E("ACT", ["pbig", "bqkv_t"], ["scrC"], lambda e, h=h, t=t: e.activation(
                scrC[:, t * TS:(t + 1) * TS], pbig[:, t * TS:(t + 1) * TS], AF.Identity,
                bias=bqkvt[:, 32 + h:33 + h]))
        for cc in range(NCH):
            E("PE", ["scrC", "i128_t"], ["ptr"], lambda e, cc=cc: e.transpose(
                ptr[:, cc * 128:cc * 128 + 128], scrC[:, cc * 128:(cc + 1) * 128], i128t.ap()))
            E("DVE", ["ptr"], ["vp%d" % h], lambda e, h=h, cc=cc: e.tensor_copy(
                vpark[:, h, cc * 128:(cc + 1) * 128], ptr[:, cc * 128:cc * 128 + 128]))
        E("ACT", ["dkb_t", "nslp_t"], ["dkh"], lambda e, h=h: e.activation(
            dkh.ap(), dkbt.ap(), AF.Exp, bias=0.0, scale=nslp[:, h:h + 1]))
        E("DVE", [], ["spk%d" % h], lambda e, h=h: e.memset(spark[:, h, :], 0.0))
        for cc in range(NCH):
            csl = slice(cc * 128, (cc + 1) * 128)
            pp = cc % 2
            kd, kdk = kdec[pp], "kdec%d" % pp
            kt, ktkk = ktk[pp], "ktk%d" % pp
            upb, upk = (pbA, "pbA") if pp == 0 else (pbC, "pbC")
            psl = slice(pp * 128, pp * 128 + 128)
            E("DVE", ["kp%d" % h, "dkh"], [kdk], lambda e, h=h, csl=csl, kd=kd: e.tensor_tensor(
                kd.ap(), kpark[:, h, csl], dkh.ap(), alu.mult))
            E("PE", [kdk, "i128_t"], ["ptr"], lambda e, kd=kd, psl=psl: e.transpose(
                ptr[:, psl], kd.ap(), i128t.ap()))
            E("DVE", ["ptr"], [ktkk], lambda e, kt=kt, psl=psl: e.tensor_copy(kt.ap(), ptr[:, psl]))
            E("PE", [ktkk, "vp%d" % h], [upk], lambda e, h=h, csl=csl, kt=kt, upb=upb: e.matmul(
                upb[:, 0:128], kt.ap(), vpark[:, h, csl], start=True, stop=True))
            E("DVE", [upk, "lamc_t", "spk%d" % h], ["spk%d" % h],
              lambda e, h=h, upb=upb: e.scalar_tensor_tensor(
                  spark[:, h, :], spark[:, h, :], lamct[:, h:h + 1], upb[:, 0:128],
                  alu.mult, alu.add))

    E("SPD", ["spk%d" % h for h in range(H)], ["wloc0"], lambda e: e.dma_start(
        out=dwloc.rearrange("(h p) d -> p h d", p=128), in_=spark.ap()), slot=sl_cb)
    E("POOL", ["wloc0"], ["wall0"], lambda e: e.collective_compute(
        "AllGather", mybir.AluOpType.bypass, replica_groups=[list(range(NCORES))],
        ins=[dwloc.ap()], outs=[dwall.ap()]))
    for h in range(H):
        E("DVE", [], ["spk%d" % h], lambda e, h=h: e.memset(spark[:, h, :], 0.0))
    for k in range(NCORES):
        E("SPD", ["wall0"] + (["wkbuf", "wvbuf"] if k == 0 else []), ["walk"],
          lambda e, k=k: e.dma_start(
              out=walk.ap(), in_=dwall[k * DM:(k + 1) * DM, :]
              .rearrange("(h p) d -> p h d", p=128)), slot=sl_cg)
        for h in range(H):
            E("DVE", ["walk", "ct_t", "spk%d" % h], ["spk%d" % h],
              lambda e, h=h, k=k: e.scalar_tensor_tensor(
                  spark[:, h, :], walk[:, h, :], ctt[:, h * 8 + k:h * 8 + k + 1],
                  spark[:, h, :], alu.mult, alu.add))

    for h in range(H):
        E("POOLD", [], ["wqbuf"], lambda e, h=h: head_w(e, wqbuf, h * HD), slot=sl_wq)
        E("ACT", ["p1s_t", "nslp_t"], ["dqh"], lambda e, h=h: e.activation(
            dqh.ap(), p1s.ap(), AF.Exp, bias=0.0, scale=nslp[:, h:h + 1]))
        E("ACT", ["diff_t", "nslp_t"], ["dmh"], lambda e, h=h: e.activation(
            dmh.ap(), difft.ap(), AF.Exp, bias=0.0, scale=nslp[:, h:h + 1]))
        E("DVE", ["dmh", "caus_t"], ["dmh"], lambda e: e.tensor_tensor(
            dmh.ap(), dmh.ap(), caust.ap(), alu.mult))
        E("ACT", ["dkb_t", "nslp_t"], ["dkh"], lambda e, h=h: e.activation(
            dkh.ap(), dkbt.ap(), AF.Exp, bias=0.0, scale=nslp[:, h:h + 1]))
        for t in range(NT):
            E("PE", ["wqbuf"] + ["xnb%d" % c for c in range(16)], ["pbB"],
              lambda e, t=t: [e.matmul(pbB[:, 0:TS], wqbuf[:, kc, :],
                                       xnb[:, kc, t * TS:(t + 1) * TS],
                                       start=(kc == 0), stop=(kc == 15)) for kc in range(16)][-1])
            E("ACT", ["pbB", "bqkv_t"], ["qth"], lambda e, h=h, t=t: e.activation(
                qth[:, t * TS:(t + 1) * TS], pbB[:, 0:TS], AF.Identity, bias=bqkvt[:, h:h + 1]))
        for cc in range(NCH):
            csl = slice(cc * 128, (cc + 1) * 128)
            pp = cc % 2
            kd, kdk = kdec[pp], "kdec%d" % pp
            kt, ktkk = ktk[pp], "ktk%d" % pp
            ms, msck = msc[pp], "msc%d" % pp
            spb, spk_ = (pbA, "pbA") if pp == 0 else (pbC, "pbC")
            psl = slice(pp * 128, pp * 128 + 128)
            E("PE", ["kp%d" % h, "qth"], [spk_], lambda e, h=h, csl=csl, spb=spb: e.matmul(
                spb[:, 0:128], kpark[:, h, csl], qth[:, csl], start=True, stop=True))
            E("DVE", [spk_, "dmh"], [msck], lambda e, spb=spb, ms=ms: e.tensor_tensor(
                ms.ap(), spb[:, 0:128], dmh.ap(), alu.mult))
            E("DVE", ["spk%d" % h], ["sbf"], lambda e, h=h: e.tensor_copy(sbf.ap(), spark[:, h, :]))
            E("DVE", ["qth", "dqh"], ["qpc"], lambda e, h=h, csl=csl: e.tensor_tensor(
                qpc.ap(), qth[:, csl], dqh.ap(), alu.mult))
            E("PE", ["vp%d" % h, msck, "sbf", "qpc"], ["pbig"],
              lambda e, h=h, csl=csl, ms=ms: (
                  e.matmul(pbig[:, csl], vpark[:, h, csl], ms.ap(), start=True, stop=False),
                  e.matmul(pbig[:, csl], sbf.ap(), qpc.ap(), start=False, stop=True))[-1])
            E("DVE", ["kp%d" % h, "dkh"], [kdk], lambda e, h=h, csl=csl, kd=kd: e.tensor_tensor(
                kd.ap(), kpark[:, h, csl], dkh.ap(), alu.mult))
            E("PE", [kdk, "i128_t"], ["ptr"], lambda e, kd=kd, psl=psl: e.transpose(
                ptr[:, 256 + psl.start:256 + psl.stop], kd.ap(), i128t.ap()))
            E("DVE", ["ptr"], [ktkk], lambda e, kt=kt, psl=psl: e.tensor_copy(
                kt.ap(), ptr[:, 256 + psl.start:256 + psl.stop]))
            E("PE", [ktkk, "vp%d" % h], ["pbig2"], lambda e, h=h, csl=csl, kt=kt: e.matmul(
                pbig2[:, 0:128], kt.ap(), vpark[:, h, csl], start=True, stop=True))
            E("DVE", ["pbig2", "lamc_t", "spk%d" % h], ["spk%d" % h],
              lambda e, h=h: e.scalar_tensor_tensor(
                  spark[:, h, :], spark[:, h, :], lamct[:, h:h + 1], pbig2[:, 0:128],
                  alu.mult, alu.add))
        E("DVE", ["pbig"], ["vp%d" % h], lambda e, h=h: e.tensor_copy(
            vpark[:, h, 0:T], pbig[:, 0:T]))

    for f in range(16):
        E("POOLD", [], ["wqbuf"], lambda e, f=f: e.dma_start(
            out=wqbuf.ap(), in_=dwout[:, f * 128:(f + 1) * 128]
            .rearrange("(kc p) g -> p kc g", p=128)), slot=sl_wo)
        for t in range(NT):
            E("PE", ["wqbuf"] + ["vp%d" % hh for hh in range(H)], ["pbA"],
              lambda e, t=t: [e.matmul(pbA[:, 0:TS], wqbuf[:, kc, :],
                                       vpark[:, kc, t * TS:(t + 1) * TS],
                                       start=(kc == 0), stop=(kc == 15)) for kc in range(16)][-1])
            E("DVE", ["pbA", "bout_t", "x%d" % f], ["x%d" % f],
              lambda e, f=f, t=t: e.scalar_tensor_tensor(
                  x_sb[:, f, t * TS:(t + 1) * TS], pbA[:, 0:TS], boutt[:, f:f + 1],
                  x_sb[:, f, t * TS:(t + 1) * TS], alu.add, alu.add))

    if upto == "l0attn":
        _finish(kb, E, x_sb, dy, sl_out, T)
        return nc, kb

    kb.barrier()
    ar_reset()
    wgs = ar("wgs", [128, 16, 4, 128], BF16)
    wus = ar("wus", [128, 16, 4, 128], BF16)
    wds = ar("wds", [128, 4, 16, 128], BF16)
    hfs = [ar("hf%d" % r, [128, T], BF16) for r in range(4)]
    hms = [ar("hm%d" % r, [128, T], BF16) for r in range(4)]
    cmp6 = [ar("cmpt%d" % i, [128, T], BF16) for i in range(6)]
    wint = [ar("wint%d" % i, [128, T], BF16) for i in range(2)]
    sl_g = kb.slot("gs")
    sl_u = kb.slot("us")
    sl_d = kb.slot("ds")

    def sparsify():
        for r in range(4):
            E("ACT", ["hf%d" % r], ["hm%d" % r], lambda e, r=r: e.activation(
                hms[r][:, 0:T], hfs[r][:, 0:T], AF.Abs))
        pairs = [(0, 1), (0, 2), (0, 3), (1, 2), (1, 3), (2, 3)]
        for i, (p, q) in enumerate(pairs):
            eng = "DVE"
            E(eng, ["hm%d" % p, "hm%d" % q], ["cmp%d" % i],
              lambda e, p=p, q=q, i=i: e.tensor_tensor(
                  cmp6[i][:, 0:T], hms[p][:, 0:T], hms[q][:, 0:T], alu.is_ge))
        E("DVE", ["cmp0", "cmp1"], ["win0"], lambda e: e.tensor_tensor(
            wint[0][:, 0:T], cmp6[0][:, 0:T], cmp6[1][:, 0:T], alu.add))
        E("DVE", ["win0", "cmp2"], ["win0"], lambda e: e.tensor_tensor(
            wint[0][:, 0:T], wint[0][:, 0:T], cmp6[2][:, 0:T], alu.add))
        E("DVE", ["win0"], ["win0"], lambda e: e.tensor_single_scalar(
            wint[0][:, 0:T], wint[0][:, 0:T], 1.5, alu.is_ge))
        E("DVE", ["win0", "hf0", "hm0"], ["hm0"], lambda e: e.tensor_tensor(
            hms[0][:, 0:T], hfs[0][:, 0:T], wint[0][:, 0:T], alu.mult))
        E("POOL", ["cmp3", "cmp4"], ["win1"], lambda e: e.tensor_tensor(
            wint[1][:, 0:T], cmp6[3][:, 0:T], cmp6[4][:, 0:T], alu.add))
        E("POOL", ["win1", "cmp0"], ["win1"], lambda e: e.tensor_tensor(
            wint[1][:, 0:T], wint[1][:, 0:T], cmp6[0][:, 0:T], alu.subtract))
        E("DVE", ["win1"], ["win1"], lambda e: e.tensor_single_scalar(
            wint[1][:, 0:T], wint[1][:, 0:T], 0.5, alu.is_ge))
        E("POOL", ["win1", "hf1", "hm1"], ["hm1"], lambda e: e.tensor_tensor(
            hms[1][:, 0:T], hfs[1][:, 0:T], wint[1][:, 0:T], alu.mult))
        E("DVE", ["cmp5", "cmp1", "win0"], ["win0"], lambda e: e.tensor_tensor(
            wint[0][:, 0:T], cmp6[5][:, 0:T], cmp6[1][:, 0:T], alu.subtract))
        E("DVE", ["win0", "cmp3"], ["win0"], lambda e: e.tensor_tensor(
            wint[0][:, 0:T], wint[0][:, 0:T], cmp6[3][:, 0:T], alu.subtract))
        E("DVE", ["win0"], ["win0"], lambda e: e.tensor_single_scalar(
            wint[0][:, 0:T], wint[0][:, 0:T], -0.5, alu.is_ge))
        E("DVE", ["win0", "hf2", "hm2"], ["hm2"], lambda e: e.tensor_tensor(
            hms[2][:, 0:T], hfs[2][:, 0:T], wint[0][:, 0:T], alu.mult))
        E("POOL", ["cmp2", "cmp4", "win1"], ["win1"], lambda e: e.tensor_tensor(
            wint[1][:, 0:T], cmp6[2][:, 0:T], cmp6[4][:, 0:T], alu.add))
        E("POOL", ["win1", "cmp5"], ["win1"], lambda e: e.tensor_tensor(
            wint[1][:, 0:T], wint[1][:, 0:T], cmp6[5][:, 0:T], alu.add))
        E("DVE", ["win1"], ["win1"], lambda e: e.tensor_single_scalar(
            wint[1][:, 0:T], wint[1][:, 0:T], 1.5, alu.is_le))
        E("POOL", ["win1", "hf3", "hm3"], ["hm3"], lambda e: e.tensor_tensor(
            hms[3][:, 0:T], hfs[3][:, 0:T], wint[1][:, 0:T], alu.mult))

    for s in range(IN0 // 512):
        E("POOLD", [], ["wgs"], lambda e, s=s: e.dma_start(
            out=wgs.ap(), in_=dwg[:, s * 512:(s + 1) * 512]
            .rearrange("(kc p) (r g) -> p kc r g", p=128, g=128)), slot=sl_g)
        E("POOLD", [], ["wus"], lambda e, s=s: e.dma_start(
            out=wus.ap(), in_=dwu[:, s * 512:(s + 1) * 512]
            .rearrange("(kc p) (r g) -> p kc r g", p=128, g=128)), slot=sl_u)
        E("POOLD", [], ["wds"], lambda e, s=s: e.dma_start(
            out=wds.ap(), in_=dwd[s * 512:(s + 1) * 512, :]
            .rearrange("(r p) (fc g) -> p r fc g", p=128, g=128)), slot=sl_d)
        for r in range(4):
            for t in range(NT):
                tsl = slice(t * TS, (t + 1) * TS)
                pp = (r * NT + t) % 2
                gpb, gpk = (pbA, "pbA") if pp == 0 else (pbC, "pbC")
                upb, upk = (pbB, "pbB") if pp == 0 else (pbig2, "pbig2")
                E("PE", ["wgs"] + ["xnb%d" % c for c in range(16)], [gpk],
                  lambda e, r=r, tsl=tsl, gpb=gpb: [e.matmul(gpb[:, 0:TS], wgs[:, kc, r, :],
                                                             xnb[:, kc, tsl], start=(kc == 0),
                                                             stop=(kc == 15)) for kc in range(16)][-1])
                E("PE", ["wus"] + ["xnb%d" % c for c in range(16)], [upk],
                  lambda e, r=r, tsl=tsl, upb=upb: [e.matmul(upb[:, 0:TS], wus[:, kc, r, :],
                                                             xnb[:, kc, tsl], start=(kc == 0),
                                                             stop=(kc == 15)) for kc in range(16)][-1])
                E("ACT", [gpk], ["scrA"], lambda e, gpb=gpb: e.activation(
                    scrA[:, 0:TS], gpb[:, 0:TS], AF.Silu))
                E("DVE", ["scrA", upk], ["hf%d" % r], lambda e, r=r, tsl=tsl, upb=upb:
                  e.tensor_tensor(hfs[r][:, tsl], scrA[:, 0:TS], upb[:, 0:TS], alu.mult))
        sparsify()
        for f in range(16):
            for t in range(NT):
                tsl = slice(t * TS, (t + 1) * TS)
                pb, pbk = (pbC, "pbC") if (f * NT + t) % 2 == 0 else (pbig, "pbig")
                E("PE", ["wds", "hm0", "hm1", "hm2", "hm3"], [pbk],
                  lambda e, f=f, tsl=tsl, pb=pb: [e.matmul(pb[:, 0:TS], wds[:, r, f, :],
                                                           hms[r][:, tsl], start=(r == 0),
                                                           stop=(r == 3)) for r in range(4)][-1])
                E("DVE", [pbk, "x%d" % f], ["x%d" % f], lambda e, f=f, tsl=tsl, pb=pb:
                  e.tensor_tensor(x_sb[:, f, tsl], x_sb[:, f, tsl], pb[:, 0:TS], alu.add))
            if f < 8:
                preconv(f, "x%d" % f)

    if upto == "l0":
        _finish(kb, E, x_sb, dy, sl_out, T)
        return nc, kb

    layernorm()

    kb.barrier()
    ar_reset()
    wdtb_off = _cur[0]
    wdtb = ar("wdtb", [128, 16, RK], BF16)
    wdob = ar("wdob", [RK, 16, 128], BF16)
    wbcb = ar("wbcb", [128, 16, 2 * NS], BF16)
    dtrt = ar("dtrt", [RK, T], BF16)
    bct_off = _cur[0]
    bct_f = ar("bct_f", [2 * NS, T], BF16)
    bpark = ar("bpark", [128, NS, T], BF16)
    cpark = ar("cpark", [128, NS, T], BF16)
    dtT = ar("dtT", [128, T], BF16)
    dtu = ar("dtu", [128, T], BF16)
    bdb = ar("bdb", [128, 2, T], BF16)
    xsc = ar("xsc", [128, 2, T], BF16)
    ssc = ar("ssc", [128, T], BF16)
    ssc2 = ar("ssc2", [128, T], BF16, at=AR_BASE + wdtb_off)
    ytl = ar("ytl", [128, T], BF16, at=AR_BASE + bct_off)
    dtTB = ar("dtTB", [128, T], BF16, at=SB_BASE + stats_off)
    sdtf = ar("sdtf", [128, 16], F32)
    if 1600 + 2 * T <= 4 * T:
        walS = ar("walS", [128, 16, NS + 1], BF16, at=SB_BASE + stMR_off)
        pkt = ar("pkt", [128, 16 * NS], F32, at=SB_BASE + stMR_off + 576)
        dtuB = ar("dtuB", [128, T], BF16, at=SB_BASE + stMR_off + 1600)
    else:
        walS = ar("walS", [128, 16, NS + 1], BF16)
        pkt = ar("pkt", [128, 16 * NS], F32)
        dtuB = ar("dtuB", [128, T], BF16)
    sl_dt = kb.slot("dtw", oneshot=True)
    sl_cb2 = kb.slot("cb2")
    sl_cg2 = kb.slot("cg2")

    E("POOLD", [], ["wdtb"], lambda e: e.dma_start(
        out=wdtb.ap(), in_=dwdtin.rearrange("(kc p) r -> p kc r", p=128)), slot=sl_dt)
    E("POOLD", [], ["wbcb"], lambda e: e.dma_start(
        out=wbcb.ap()[:, :, 0:NS], in_=dwb.rearrange("(kc p) n -> p kc n", p=128)), slot=sl_dt)
    E("POOLD", [], ["wbcb"], lambda e: e.dma_start(
        out=wbcb.ap()[:, :, NS:2 * NS], in_=dwc.rearrange("(kc p) n -> p kc n", p=128)), slot=sl_dt)
    E("POOLD", [], ["wdob"], lambda e: e.dma_start(
        out=wdob.ap(), in_=dwdtout.rearrange("k (fc p) -> k fc p", p=128)), slot=sl_dt)

    for t in range(NT):
        E("PE", ["wdtb"] + ["xnb%d" % c for c in range(16)], ["pbig"],
          lambda e, t=t: [e.matmul(pbig[0:RK, t * TS:(t + 1) * TS], wdtb[:, kc, :],
                                   xnb[:, kc, t * TS:(t + 1) * TS], start=(kc == 0),
                                   stop=(kc == 15)) for kc in range(16)][-1])
    E("DVE", ["pbig"], ["dtrt"], lambda e: e.tensor_copy(dtrt[:, 0:T], pbig[0:RK, 0:T]))
    for t in range(NT):
        E("PE", ["wbcb"] + ["xnb%d" % c for c in range(16)], ["pbA"],
          lambda e, t=t: [e.matmul(pbA[0:2 * NS, 0:TS], wbcb[:, kc, :],
                                   xnb[:, kc, t * TS:(t + 1) * TS], start=(kc == 0),
                                   stop=(kc == 15)) for kc in range(16)][-1])
        E("DVE", ["pbA"], ["bct_f"], lambda e, t=t: e.tensor_copy(
            bct_f[:, t * TS:(t + 1) * TS], pbA[0:2 * NS, 0:TS]))

    sl_bc = kb.slot("bcst", oneshot=True)
    E("POOLD", ["bct_f"], ["dbc"], lambda e: e.dma_start(
        out=dbc[:, 0:T], in_=bct_f.ap()), slot=sl_bc)

    def bcast_park(park, parkkey, row0):
        for n in range(NS):
            E("POOLD", ["dbc"], [parkkey], lambda e, n=n: e.dma_start(
                out=park[:, n, :], in_=dbc[row0 + n:row0 + n + 1, 0:T]
                .partition_broadcast(128)), slot=sl_bc)

    bcast_park(bpark, "bpark", 0)
    bcast_park(cpark, "cpark", NS)
    if upto == "ssmproj":
        _finish(kb, E, x_sb, dy, sl_out, T)
        return nc, kb

    dtTs, dtus = [dtT, dtTB], [dtu, dtuB]

    def dt_chunk(hc):
        p = hc % 2
        dT, dU, kT, kU = dtTs[p], dtus[p], "dtT%d" % p, "dtu%d" % p
        for t in range(NT):
            tsl = slice(t * TS, (t + 1) * TS)
            E("PE", ["dtrt", "wdob"], ["pbB"], lambda e, hc=hc, t=t: e.matmul(
                pbB[:, 0:TS], wdob[:, hc, :], dtrt[:, t * TS:(t + 1) * TS],
                start=True, stop=True))
            E("ACT", ["pbB", "bdt_t"], ["scrA"], lambda e, hc=hc: e.activation(
                scrA[:, 0:TS], pbB[:, 0:TS], AF.Abs, bias=bdtt[:, hc:hc + 1]))
            E("ACT", ["scrA"], ["scrA"], lambda e: e.activation(
                scrA[:, 0:TS], scrA[:, 0:TS], AF.Exp, bias=0.0, scale=-1.0))
            E("ACT", ["scrA"], ["scrA"], lambda e: e.activation(
                scrA[:, 0:TS], scrA[:, 0:TS], AF.Ln, bias=1.0))
            E("DVE", ["pbB", "bdt_t"], ["scrB"], lambda e, hc=hc: e.tensor_scalar(
                scrB[:, 0:TS], pbB[:, 0:TS], bdtt[:, hc:hc + 1], 0.0, alu.add, alu.max))
            E("DVE", ["scrA", "scrB"], [kT], lambda e, tsl=tsl, dT=dT: e.tensor_tensor(
                dT[:, tsl], scrA[:, 0:TS], scrB[:, 0:TS], alu.add))
            E("DVE", [kT, "xnb%d" % hc], [kU], lambda e, hc=hc, tsl=tsl, dT=dT, dU=dU:
              e.tensor_tensor(dU[:, tsl], dT[:, tsl], xnb[:, hc, tsl], alu.mult))

    if upto != "ssmdt":
        dt_chunk(0)
    for hc in (range(16) if upto != "ssmdt" else []):
        p = hc % 2
        dT, kT, kU = dtTs[p], "dtT%d" % p, "dtu%d" % p
        E("DVE", [kT], ["wst"], lambda e, hc=hc, dT=dT: e.reduce_sum(
            wst[:, hc, NS:NS + 1], dT[:, 0:T], axis=mybir.AxisListType.X))
        for n in range(NS):
            par = n % 2
            adp = (pbig, "pbig") if par == 0 else (pbig2, "pbig2")
            E("POOL", [kU, "bpark"], ["bdb%d" % par], lambda e, n=n, par=par, p=p:
              e.tensor_tensor(bdb[:, par, :], dtus[p][:, 0:T], bpark[:, n, :], alu.mult))
            E("ACT", [kT, "a_t"], [adp[1]], lambda e, hc=hc, n=n, adp=adp, dT=dT: e.activation(
                adp[0][:, 0:T], dT[:, 0:T], AF.Exp, bias=0.0,
                scale=asbt[:, hc * NS + n:hc * NS + n + 1]))
            E("DVE", [adp[1], "bdb%d" % par], ["xsc%d" % par],
              lambda e, par=par, adp=adp: e.tensor_tensor_scan(
                  xsc[:, par, :], adp[0][:, 0:T], bdb[:, par, :], 0.0, alu.mult, alu.add))
            E("DVE", ["xsc%d" % par, "wst"], ["wst"], lambda e, hc=hc, n=n, par=par: e.tensor_copy(
                wst[:, hc, n:n + 1], xsc[:, par, T - 1:T]))
            if n == 2 and hc + 1 < 16:
                dt_chunk(hc + 1)
    if upto == "ssmdt":
        for hc in range(2):
            dt_chunk(hc)
        _finish(kb, E, x_sb, dy, sl_out, T)
        return nc, kb
    if upto == "ssmp1":
        _finish(kb, E, x_sb, dy, sl_out, T)
        return nc, kb
    E("SPD", ["wst"], ["sloc1"], lambda e: e.dma_start(
        out=dsloc.rearrange("(hc p) c -> p hc c", p=128), in_=wst.ap()), slot=sl_cb2)
    E("POOL", ["sloc1"], ["sall1"], lambda e: e.collective_compute(
        "AllGather", mybir.AluOpType.bypass, replica_groups=[list(range(NCORES))],
        ins=[dsloc.ap()], outs=[dsall.ap()]))
    E("DVE", [], ["est"], lambda e: e.memset(est.ap(), 0.0))
    for k in range(NCORES):
        E("POOLD", ["sall1"], ["walS"], lambda e, k=k: e.dma_start(
            out=walS.ap(), in_=dsall[k * DM:(k + 1) * DM, :]
            .rearrange("(hc p) c -> p hc c", p=128)), slot=sl_cg2)
        E("DVE", ["walS"], ["sdtf"], lambda e: e.tensor_copy(
            sdtf.ap(), walS[:, :, NS]))
        for hc in range(16):
            E("DVE", ["sdtf", "a_t", "mask8_t"], ["pkt"], lambda e, k=k, hc=hc:
              e.tensor_scalar(pkt[:, hc * NS:(hc + 1) * NS],
                              asbt[:, hc * NS:(hc + 1) * NS],
                              sdtf[:, hc:hc + 1], mask8t[:, k:k + 1],
                              alu.mult, alu.mult))
        E("ACT", ["pkt"], ["pkt"], lambda e: e.activation(pkt.ap(), pkt.ap(), AF.Exp))
        E("DVE", ["est", "pkt"], ["est"], lambda e: e.tensor_tensor(
            est.ap(), est.ap(), pkt.ap(), alu.mult))
        E("DVE", ["walS", "mask8_t", "est"], ["est"], lambda e, k=k: e.scalar_tensor_tensor(
            est.ap().rearrange("p (hc n) -> p hc n", n=NS), walS.ap()[:, :, 0:NS],
            mask8t[:, k:k + 1], est.ap().rearrange("p (hc n) -> p hc n", n=NS),
            alu.mult, alu.add))
    dt_chunk(0)
    for hc in range(16):
        p = hc % 2
        dT, kT, kU = dtTs[p], "dtT%d" % p, "dtu%d" % p
        pend = None
        for n in range(NS):
            par = n % 2
            adp = (pbig, "pbig") if par == 0 else (pbig2, "pbig2")
            E("POOL", [kU, "bpark"], ["bdb%d" % par], lambda e, n=n, par=par, p=p:
              e.tensor_tensor(bdb[:, par, :], dtus[p][:, 0:T], bpark[:, n, :], alu.mult))
            E("ACT", [kT, "a_t"], [adp[1]], lambda e, hc=hc, n=n, adp=adp, dT=dT: e.activation(
                adp[0][:, 0:T], dT[:, 0:T], AF.Exp, bias=0.0,
                scale=asbt[:, hc * NS + n:hc * NS + n + 1]))
            E("DVE", [adp[1], "bdb%d" % par, "est"], ["xsc%d" % par],
              lambda e, hc=hc, n=n, par=par, adp=adp: e.tensor_tensor_scan(
                  xsc[:, par, :], adp[0][:, 0:T], bdb[:, par, :],
                  est[:, hc * NS + n:hc * NS + n + 1], alu.mult, alu.add))
            if n == 0:
                E("DVE", ["xsc0", "cpark"], ["ytl"] + (["bct_f"] if hc == 0 else []),
                  lambda e: e.tensor_tensor(ytl[:, 0:T], xsc[:, 0, :], cpark[:, 0, :],
                                            alu.mult))
            else:
                sscb, ssck = (ssc, "ssc") if par == 0 else (ssc2, "ssc2")
                E("DVE", ["xsc%d" % par, "cpark"], [ssck], lambda e, n=n, par=par, sscb=sscb:
                  e.tensor_tensor(sscb[:, 0:T], xsc[:, par, :], cpark[:, n, :], alu.mult))
                if pend is not None:
                    sb_, sk_ = pend
                    E("POOL", [sk_, "ytl"], ["ytl"], lambda e, sb_=sb_: e.tensor_tensor(
                        ytl[:, 0:T], ytl[:, 0:T], sb_[:, 0:T], alu.add))
                pend = (sscb, ssck)
            if n == 2 and hc + 1 < 16:
                dt_chunk(hc + 1)
        if pend is not None:
            sb_, sk_ = pend
            E("POOL", [sk_, "ytl"], ["ytl"], lambda e, sb_=sb_: e.tensor_tensor(
                ytl[:, 0:T], ytl[:, 0:T], sb_[:, 0:T], alu.add))
        E("DVE", ["ytl", "dvec_t", "xnb%d" % hc], ["ytl"], lambda e, hc=hc: e.scalar_tensor_tensor(
            ytl[:, 0:T], xnb[:, hc, :], dvect[:, hc:hc + 1], ytl[:, 0:T], alu.mult, alu.add))
        E("DVE", ["ytl", "x%d" % hc], ["x%d" % hc], lambda e, hc=hc: e.tensor_tensor(
            x_sb[:, hc, :], x_sb[:, hc, :], ytl[:, 0:T], alu.add))

    if upto == "ssm":
        _finish(kb, E, x_sb, dy, sl_out, T)
        return nc, kb

    kb.barrier()
    ar_reset()
    suh_off = _cur[0]
    suh = ar("suh", [128, 16, 128], BF16)
    suhn_off = _cur[0]
    suhn = ar("suhn", [128, 16, 128], BF16)
    z2f = ar("z2f", [128, 16, T], BF16)
    midB = ar("midB", [128, 32, TH], BF16)
    w1gs_off = _cur[0]
    w1gs = ar("w1gs", [128, 16, 128], BF16)
    w1us = ar("w1us", [128, 16, 128], BF16)
    w1ds_off = _cur[0]
    w1ds = ar("w1ds", [128, 2, 16, 128], BF16)
    su2t = ar("su2t", [128, 128], BF16)
    su2tB = ar("su2tB", [128, 128], BF16)
    if 8 * TH <= 4 * T:
        btA = ar("btA", [128, 4, TH], BF16, at=SB_BASE + stMR_off)
    else:
        btA = ar("btA", [128, 4, TH], BF16)
    btB = ar("btB", [128, 4, TH], BF16, at=AR_BASE + suh_off)
    w1gsB = ar("w1gsB", [128, 16, 128], BF16, at=AR_BASE + suhn_off)
    w1usB = ar("w1usB", [128, 16, 128], BF16, at=AR_BASE + w1ds_off)
    w1dsB = ar("w1dsB", [128, 2, 16, 128], BF16, at=AR_BASE + w1gs_off)
    sl_g1 = kb.slot("g1s")
    sl_g1b = kb.slot("g1sb")
    sl_u1 = kb.slot("u1s")
    sl_u1b = kb.slot("u1sb")
    sl_d1 = kb.slot("d1s")
    sl_d1b = kb.slot("d1sb")
    H16M = _hadamard(16)

    for c in range(16):
        E("DVE", ["h128_t", "suug_t"], ["suh"], lambda e, c=c: e.tensor_scalar(
            suh[:, c, :], h128t.ap(), suugt[:, c:c + 1], None, alu.mult))
        E("DVE", ["suh"], ["suhn"], lambda e, c=c: e.tensor_scalar(
            suhn[:, c, :], suh[:, c, :], -1.0, None, alu.mult))
    for a2 in range(16):
        for t in range(NT):
            tg = slice(t * TS, (t + 1) * TS)
            E("PE", ["suh", "suhn"] + ["xnb%d" % c for c in range(16)], ["pbA"],
              lambda e, a2=a2, tg=tg: [e.matmul(
                  pbA[:, 0:TS], (suh if H16M[a2, q] > 0 else suhn)[:, q, :],
                  xnb[:, q, tg], start=(q == 0), stop=(q == 15))
                  for q in range(16)][-1])
            E("ACT", ["pbA"], ["z2f"], lambda e, a2=a2, tg=tg: e.activation(
                z2f[:, a2, tg], pbA[:, 0:TS], AF.Copy))
    kb.barrier()
    if NHALF >= 2:
        midA = ar("midA", [128, 32, TH], BF16, at=xnb_off)
    else:
        midA = ar("midA", [128, 32, TH], BF16)

    def midt(a):
        return midA[:, a, :] if a < 32 else midB[:, a - 32, :]

    for half in range(NHALF):
        hb = half * TH
        for a in range(64):
            pp = a % 2
            gb = w1gs if pp == 0 else w1gsB
            gk = "w1gs" if pp == 0 else "w1gsB"
            gsl = sl_g1 if pp == 0 else sl_g1b
            guok = ["gu_ok%d" % (half - 1)] if (half > 0 and a == 0) else []
            gwr = [gk] + (["suhn"] if (a == 1 and half == 0) else [])
            E("SPD", guok + ["dw1gb"], gwr, lambda e, a=a, gb=gb: e.dma_start(
                out=gb.ap(), in_=dw1gb[:, a * 128:(a + 1) * 128]
                .rearrange("(kc p) g -> p kc g", p=128)), slot=gsl)
            ub = w1us if pp == 0 else w1usB
            uk = "w1us" if pp == 0 else "w1usB"
            usl = sl_u1 if pp == 0 else sl_u1b
            E("SPD", guok + ["dw1ub"], [uk], lambda e, a=a, ub=ub: e.dma_start(
                out=ub.ap(), in_=dw1ub[:, a * 128:(a + 1) * 128]
                .rearrange("(kc p) g -> p kc g", p=128)), slot=usl)
            for t in range(NTH):
                tz = slice(hb + t * TSH, hb + (t + 1) * TSH)
                tloc = slice(t * TSH, (t + 1) * TSH)
                E("PE", [gk, "z2f"], ["pbB"],
                  lambda e, tz=tz, gb=gb: [e.matmul(pbB[:, 0:TSH], gb[:, kc, :],
                                                    z2f[:, kc, tz], start=(kc == 0),
                                                    stop=(kc == 15)) for kc in range(16)][-1])
                E("PE", [uk, "z2f"], ["pbC"],
                  lambda e, tz=tz, ub=ub: [e.matmul(pbC[:, 0:TSH], ub[:, kc, :],
                                                    z2f[:, kc, tz], start=(kc == 0),
                                                    stop=(kc == 15)) for kc in range(16)][-1])
                E("ACT", ["pbB", "wscg_t"], ["scrA"], lambda e, a=a: e.activation(
                    scrA[:, 0:TSH], pbB[:, 0:TSH], AF.Silu, bias=0.0,
                    scale=wscgt[:, a:a + 1]))
                E("DVE", ["pbC", "wscu_t"], ["scrB"], lambda e, a=a: e.tensor_scalar(
                    scrB[:, 0:TSH], pbC[:, 0:TSH], wscut[:, a:a + 1], None, alu.mult))
                E("DVE", ["scrA", "scrB"], ["mid%d" % a], lambda e, a=a, tloc=tloc:
                  e.tensor_tensor(midt(a)[:, tloc], scrA[:, 0:TSH], scrB[:, 0:TSH], alu.mult))
        for a in range(64):
            st2, sk2 = (su2t, "su2t") if a % 2 == 0 else (su2tB, "su2tB")
            ppb, ppk = (pbig, "pbig") if a % 2 == 0 else (pbig2, "pbig2")
            E("DVE", ["h128_t", "sudp_t"], [sk2], lambda e, a=a, st2=st2: e.tensor_scalar(
                st2.ap(), h128t.ap(), sudpt[:, a:a + 1], None, alu.mult))
            for t in range(NTH):
                tloc = slice(t * TSH, (t + 1) * TSH)
                E("PE", [sk2, "mid%d" % a], [ppk], lambda e, a=a, tloc=tloc, st2=st2, ppb=ppb:
                  e.matmul(ppb[:, 0:TSH], st2.ap(), midt(a)[:, tloc], start=True, stop=True))
                E("ACT", [ppk], ["mid%d" % a], lambda e, a=a, tloc=tloc, ppb=ppb: e.activation(
                    midt(a)[:, tloc], ppb[:, 0:TSH], AF.Copy))
        for st in range(3):
            stp = 4 ** st
            gord = [0, 4, 8, 12, 1, 5, 9, 13, 2, 6, 10, 14, 3, 7, 11, 15] \
                if st < 2 else list(range(16))
            for gi, g in enumerate(gord):
                lo = g % stp
                hi = g // stp
                base = hi * stp * 4 + lo
                ia, ib, ic, id_ = base, base + stp, base + 2 * stp, base + 3 * stp
                bt = btA if gi % 2 == 0 else btB
                bk = "btA%d" if gi % 2 == 0 else "btB%d"
                ex = ["suh"] if (half == 0 and st == 0 and gi == 1) else []
                tspec = [("DVE", 0, ia, ib, alu.add), ("POOL", 1, ia, ib, alu.subtract),
                         ("DVE", 2, ic, id_, alu.add), ("POOL", 3, ic, id_, alu.subtract)]
                for eng, ti, p, q, op in tspec:
                    E(eng, ["mid%d" % p, "mid%d" % q], [bk % ti] + ex,
                      lambda e, bt=bt, ti=ti, p=p, q=q, op=op: e.tensor_tensor(
                          bt[:, ti, 0:TH], midt(p)[:, 0:TH], midt(q)[:, 0:TH], op))
                    ex = []
                ospec = [("DVE", ia, 0, 2, alu.add), ("POOL", ib, 1, 3, alu.add),
                         ("DVE", ic, 0, 2, alu.subtract), ("DVE", id_, 1, 3, alu.subtract)]
                for eng, o, t1, t2, op in ospec:
                    E(eng, [bk % t1, bk % t2], ["mid%d" % o],
                      lambda e, bt=bt, o=o, t1=t1, t2=t2, op=op: e.tensor_tensor(
                          midt(o)[:, 0:TH], bt[:, t1, 0:TH], bt[:, t2, 0:TH], op))
        E("DVE", [], ["dn_ok%d" % half, "w1gs", "w1us", "w1usB"],
          lambda e: e.engine_nop())
        for s in range(32):
            wb, wk, dsl = (w1ds, "w1ds", sl_d1) if s % 2 == 0 else (w1dsB, "w1dsB", sl_d1b)
            E("SPD", ["dn_ok%d" % half, "dw1db"], [wk], lambda e, s=s, wb=wb: e.dma_start(
                out=wb.ap(), in_=dw1db[s * 256:(s + 1) * 256, :]
                .rearrange("(r p) (fc g) -> p r fc g", p=128, g=128)), slot=dsl)
            for f in range(16):
                for t in range(NTH):
                    tloc = slice(t * TSH, (t + 1) * TSH)
                    tg = slice(hb + t * TSH, hb + (t + 1) * TSH)
                    pb, pbk = [(pbC, "pbC"), (pbB, "pbB"), (pbA, "pbA"),
                               (pbig, "pbig")][(f + t) % 4]
                    E("PE", [wk, "mid%d" % (s * 2), "mid%d" % (s * 2 + 1)], [pbk],
                      lambda e, s=s, f=f, tloc=tloc, pb=pb, wb=wb: [e.matmul(
                          pb[:, 0:TSH], wb[:, r, f, :], midt(s * 2 + r)[:, tloc],
                          start=(r == 0), stop=(r == 1)) for r in range(2)][-1])
                    if (s + f) % 2 == 0:
                        E("DVE", [pbk, "wscdp_t", "x%d" % f], ["x%d" % f],
                          lambda e, f=f, tg=tg, pb=pb: e.scalar_tensor_tensor(
                              x_sb[:, f, tg], pb[:, 0:TSH], wscdpt[:, f:f + 1],
                              x_sb[:, f, tg], alu.mult, alu.add))
                    else:
                        sc, sck = (scrA, "scrA") if f % 2 == 0 else (scrB, "scrB")
                        E("ACT", [pbk, "wscdp_t"], [sck], lambda e, f=f, pb=pb, sc=sc:
                          e.activation(sc[:, 0:TSH], pb[:, 0:TSH], AF.Identity,
                                       bias=0.0, scale=wscdpt[:, f:f + 1]))
                        E("POOL", [sck, "x%d" % f], ["x%d" % f],
                          lambda e, f=f, tg=tg, sc=sc: e.tensor_tensor(
                              x_sb[:, f, tg], x_sb[:, f, tg], sc[:, 0:TSH], alu.add))
        E("DVE", [], ["gu_ok%d" % half, "w1ds", "w1dsB"], lambda e: e.engine_nop())

    _finish(kb, E, x_sb, dy, sl_out, T)
    return nc, kb


def _finish(kb, E, x_sb, dy, sl_out, T):
    for c in range(16):
        E("SPD", ["x%d" % c], [], lambda e, c=c: e.dma_start(
            out=dy[c * 128:(c + 1) * 128, :], in_=x_sb[:, c, :]), slot=sl_out)
    kb.barrier()


def prep_shared(ii):
    f32 = np.float32
    C = np.ascontiguousarray
    d = {}
    d["wqkvT"] = C(np.asarray(ii["L0_attn_w_qkv"]).T.astype(f32))
    d["woutT"] = C(np.asarray(ii["L0_attn_w_out"]).T.astype(f32))
    d["bqkv"] = C(np.asarray(ii["L0_attn_b_qkv"]).reshape(48, 128).T.astype(f32))
    d["bout"] = C(np.asarray(ii["L0_attn_b_out"]).reshape(16, 128).T.astype(f32))
    perm = (4 * np.arange(2048)[None, :] + np.arange(4)[:, None]).ravel()
    d["wg_p"] = C(np.asarray(ii["L0_ffn_w_gate"])[:, perm].astype(f32))
    d["wu_p"] = C(np.asarray(ii["L0_ffn_w_up"])[:, perm].astype(f32))
    d["wd_p"] = C(np.asarray(ii["L0_ffn_w_down"])[perm, :].astype(f32))
    d["wdtinT"] = C(np.asarray(ii["L1_attn_W_dt_in"]).T.astype(f32))
    d["wdtoutT"] = C(np.asarray(ii["L1_attn_W_dt_out"]).T.astype(f32))
    d["wbT"] = C(np.asarray(ii["L1_attn_W_B"]).T.astype(f32))
    d["wcT"] = C(np.asarray(ii["L1_attn_W_C"]).T.astype(f32))
    d["alog_l"] = C(np.asarray(ii["L1_attn_A_log"]).reshape(16, 128, NS)
                    .transpose(1, 0, 2).reshape(128, 16 * NS).astype(f32))
    d["bdt"] = C(np.asarray(ii["L1_attn_b_dt_out"]).reshape(16, 128).T.astype(f32))
    d["dvec"] = C(np.asarray(ii["L1_attn_D"]).reshape(16, 128).T.astype(f32))
    d["su_ug"] = C(np.asarray(ii["L1_ffn_SU_ug"]).reshape(16, 128).T.astype(f32))
    d["su_dp"] = C(np.asarray(ii["L1_ffn_SU_dp"]).reshape(64, 128).T.astype(f32))
    wsc = np.asarray(ii["L1_ffn_Wscale_ug"])
    d["wscu"] = C(wsc[:IN1].reshape(64, 128).T.astype(f32))
    d["wscg"] = C(wsc[IN1:].reshape(64, 128).T.astype(f32))
    d["wscdp"] = C(np.asarray(ii["L1_ffn_Wscale_dp"]).reshape(16, 128).T.astype(f32))
    d["w1upT"] = C(np.asarray(ii["L1_ffn_W_up"]).T.astype(f32))
    d["w1gateT"] = C(np.asarray(ii["L1_ffn_W_gate"]).T.astype(f32))
    d["w1downT"] = C(np.asarray(ii["L1_ffn_W_down"]).T.astype(f32))
    slopes = _slopes32(H)
    d["nslopes_b"] = C(np.broadcast_to(-slopes[None, :], (128, 16)).astype(f32))
    pos = np.arange(128, dtype=np.float64)
    d["dkb"] = C(np.broadcast_to((127.0 - pos)[None, :], (128, 128)).astype(f32))
    diff = pos[None, :] - pos[:, None]
    d["diffc"] = C(np.maximum(diff, 0.0).astype(f32))
    d["causal"] = C((diff >= 0).astype(f32))
    d["h128"] = C(_hadamard(128).astype(f32))
    d["i128"] = C(np.eye(128, dtype=f32))
    return d


def prep_core(ii, core, T, shared):
    f32 = np.float32
    C = np.ascontiguousarray
    d = dict(shared)
    x = np.asarray(ii["L0_attn_x"])[0]
    d["xT"] = C(x[core * T:(core + 1) * T, :].T.astype(f32))
    tpos = np.arange(128, dtype=np.float64)
    d["p1b"] = C(np.broadcast_to((tpos + 1.0)[None, :], (128, 128)).astype(f32))
    slopes = _slopes32(H).astype(np.float64)
    ct = np.zeros((16, 8), dtype=np.float64)
    for h in range(H):
        lcc = math.exp(-slopes[h] * T)
        for k in range(core):
            ct[h, k] = lcc ** (core - 1 - k)
    d["ct_l0"] = C(np.broadcast_to(ct.reshape(1, 128), (128, 128)).astype(f32))
    m8 = np.zeros(8, dtype=f32)
    m8[:core] = 1.0
    d["mask8"] = C(np.broadcast_to(m8[None, :], (128, 8)).astype(f32))
    return d


def _slopes32(n_heads):
    start = np.float32(2.0) ** np.float32(-(2.0 ** -(math.log2(n_heads) - 3)))
    return np.array([start ** np.float32(i + 1) for i in range(n_heads)],
                    dtype=np.float32)



_CACHE = {}


def kernel(**inputs):
    from concourse.bass_utils import run_bass_kernel_spmd
    T = 1024
    if "nc" not in _CACHE:
        _CACHE["nc"] = build_nc(T, "full")
    nc, kb = _CACHE["nc"]
    shared = prep_shared(inputs)
    in_maps = [prep_core(inputs, c, T, shared) for c in range(NCORES)]
    res = run_bass_kernel_spmd(nc, in_maps, core_ids=list(range(NCORES)))
    out = np.concatenate(
        [np.ascontiguousarray(np.asarray(r["yT"]).T) for r in res.results], axis=0)
    return out[None].astype(np.float32)


# version 14
# speedup vs baseline: 2.3365x; 2.3365x over previous
import math
import numpy as np
import concourse.bass as bass
import concourse.mybir as mybir
from concourse.alu_op_type import AluOpType as alu

F32 = mybir.dt.float32
BF16 = mybir.dt.bfloat16
_PSUM_KEYS = frozenset(["pbA", "pbB", "pbC", "pbig", "pbig2", "ptr"])
AF = mybir.ActivationFunctionType
NCORES = 8
H, HD, CH, DM, IN0, IN1, NS, RK = 16, 128, 128, 2048, 8192, 8192, 16, 64


class Sem:
    def __init__(self, name, sem, step, oneshot=False):
        self.name, self.sem, self.step, self.oneshot, self.n = name, sem, step, oneshot, 0
        self.owner = None


class Lane:
    def __init__(self, name, eng, sem):
        self.name, self.eng, self.s, self.waited = name, eng, sem, {}


class KB:
    def __init__(self, nc):
        self.nc = nc
        self.lanes, self.sems = {}, {}
        for nm, eng in (("PE", nc.tensor), ("DVE", nc.vector), ("ACT", nc.scalar),
                        ("POOL", nc.gpsimd)):
            self.lanes[nm] = Lane(nm, eng, self._sem(nm, 1))
        for nm, eng in (("SPD", nc.sync), ("POOLD", nc.gpsimd)):
            self.lanes[nm] = Lane(nm, eng, None)
        self.reg = {}
        self._consts = {}
        self.nops = 0

    def _sem(self, name, step):
        h = self.nc.ctx.enter_context(self.nc.semaphore("s_" + name))
        s = Sem(name, h, step)
        self.sems[name] = s
        return s

    def slot(self, name, oneshot=False):
        s = self._sem(name, 16)
        s.oneshot = oneshot
        return s

    def emit(self, lane_name, reads, writes, fn, slot=None):
        L = self.lanes[lane_name]
        if self._consts and slot is None and "cpool" not in reads:
            reads = list(reads) + ["cpool"]
        s = slot if slot is not None else L.s
        assert s is not None, f"lane {lane_name} needs a slot"
        if s.step == 16:
            assert s.owner in (None, lane_name), f"slot {s.name}: {lane_name} vs {s.owner}"
            s.owner = lane_name
        toks = []
        for r in reads:
            e = self.reg.get(r)
            if e and e["w"]:
                toks.append(e["w"])
            if e and r in _PSUM_KEYS:
                toks.extend(t for t in e["r"] if t[0] is not s)
        for w in writes:
            e = self.reg.get(w)
            if e:
                if e["w"]:
                    toks.append(e["w"])
                toks.extend(e["r"])
        need = {}
        for (ss, v) in toks:
            if v > need.get(ss.name, 0):
                need[ss.name] = v
        for nm, v in sorted(need.items()):
            ss = self.sems[nm]
            if ss.step == 16 and v != ss.n:
                if ss.oneshot:
                    v = ss.n
                else:
                    raise AssertionError(f"stale DMA wait {nm}: {v} != {ss.n}")
            if L.waited.get(nm, -1) >= v:
                continue
            L.eng.wait_ge(ss.sem, v)
            L.waited[nm] = v
        inst = fn(L.eng)
        inst.then_inc(s.sem, s.step)
        s.n += s.step
        self.nops += 1
        tok = (s, s.n)
        for w in writes:
            self.reg[w] = {"w": tok, "r": []}
        for r in reads:
            self.reg.setdefault(r, {"w": None, "r": []})["r"].append(tok)
        return tok

    def barrier(self):
        for a in ("PE", "DVE", "ACT", "POOL", "SPD"):
            La = self.lanes[a]
            for s in self.sems.values():
                if s.n > 0 and La.waited.get(s.name, -1) < s.n:
                    La.eng.wait_ge(s.sem, s.n)
                    La.waited[s.name] = s.n

    def init_consts(self, values, alloc=None):
        values = [float(v) for v in values]
        alloc = alloc or self.nc.alloc_sbuf_tensor
        cp = alloc("cpool", [128, len(values)], F32)
        self.cpool = cp
        def fill(e):
            last = None
            for i, v in enumerate(values):
                last = e.memset(cp[:, i:i + 1], v)
                self.nc.const_aps.register(F32, v, cp[:, i:i + 1])
                self._consts[v] = cp[:, i:i + 1]
            return last
        self.emit("DVE", [], ["cpool"], fill)


def _hadamard(n):
    h = np.array([[1.0]])
    while h.shape[0] < n:
        h = np.block([[h, h], [h, -h]])
    return h


def build_nc(T=1024, upto="full"):
    assert T % CH == 0
    NCH = T // CH
    TS = min(512, T)
    NT = T // TS
    TH = T // 2 if T >= 256 else T
    NHALF = T // TH
    TSH = min(TS, TH)
    NTH = TH // TSH
    nc = bass.Bass("TRN2", target_bir_lowering=False)
    kb = KB(nc)
    E = kb.emit
    DT = nc.dram_tensor

    dx = DT("xT", [DM, T], F32, kind="ExternalInput")
    dy = DT("yT", [DM, T], F32, kind="ExternalOutput")
    dwqkv = DT("wqkvT", [DM, 3 * DM], F32, kind="ExternalInput")
    dwout = DT("woutT", [DM, DM], F32, kind="ExternalInput")
    dbqkv = DT("bqkv", [128, 48], F32, kind="ExternalInput")
    dbout = DT("bout", [128, 16], F32, kind="ExternalInput")
    dwg = DT("wg_p", [DM, IN0], F32, kind="ExternalInput")
    dwu = DT("wu_p", [DM, IN0], F32, kind="ExternalInput")
    dwd = DT("wd_p", [IN0, DM], F32, kind="ExternalInput")
    dwdtin = DT("wdtinT", [DM, RK], F32, kind="ExternalInput")
    dwdtout = DT("wdtoutT", [RK, DM], F32, kind="ExternalInput")
    dwb = DT("wbT", [DM, NS], F32, kind="ExternalInput")
    dwc = DT("wcT", [DM, NS], F32, kind="ExternalInput")
    dalog = DT("alog_l", [128, 16 * NS], F32, kind="ExternalInput")
    dbdt = DT("bdt", [128, 16], F32, kind="ExternalInput")
    ddvec = DT("dvec", [128, 16], F32, kind="ExternalInput")
    dsuug = DT("su_ug", [128, 16], F32, kind="ExternalInput")
    dsudp = DT("su_dp", [128, 64], F32, kind="ExternalInput")
    dwscu = DT("wscu", [128, 64], F32, kind="ExternalInput")
    dwscg = DT("wscg", [128, 64], F32, kind="ExternalInput")
    dwscdp = DT("wscdp", [128, 16], F32, kind="ExternalInput")
    dw1u = DT("w1upT", [DM, IN1], F32, kind="ExternalInput")
    dw1g = DT("w1gateT", [DM, IN1], F32, kind="ExternalInput")
    dw1d = DT("w1downT", [IN1, DM], F32, kind="ExternalInput")
    dw1ub = DT("w1ub", [DM, IN1], BF16)
    dw1gb = DT("w1gb", [DM, IN1], BF16)
    dw1db = DT("w1db", [IN1, DM], BF16)
    dnslp = DT("nslopes_b", [128, 16], F32, kind="ExternalInput")
    dp1 = DT("p1b", [128, 128], F32, kind="ExternalInput")
    ddk = DT("dkb", [128, 128], F32, kind="ExternalInput")
    ddiff = DT("diffc", [128, 128], F32, kind="ExternalInput")
    dcaus = DT("causal", [128, 128], F32, kind="ExternalInput")
    dh128 = DT("h128", [128, 128], F32, kind="ExternalInput")
    di128 = DT("i128", [128, 128], F32, kind="ExternalInput")
    dct = DT("ct_l0", [128, 128], F32, kind="ExternalInput")
    dmask8 = DT("mask8", [128, 8], F32, kind="ExternalInput")
    dwloc = DT("wloc0", [H * HD, HD], F32)
    dwall = DT("wall0", [NCORES * H * HD, HD], F32)
    dsloc = DT("sloc1", [DM, NS + 1], F32)
    dbc = DT("dbc", [2 * NS, 1024], BF16)
    dsall = DT("sall1", [NCORES * DM, NS + 1], F32)

    _pc = [0]

    def A(name, shape, dtype):
        sz = int(np.prod(shape[1:])) * (2 if dtype == BF16 else (1 if dtype == mybir.dt.uint8 else 4))
        _pc[0] = (_pc[0] + 31) // 32 * 32
        t = nc.alloc_sbuf_tensor(name, shape, dtype)
        _pc[0] += sz
        return t
    x_sb = A("x_sb", [128, 16, T], F32)
    SB_BASE = None
    for _al in nc.m.functions[0].allocations:
        if isinstance(_al, mybir.MemoryLocationSet) and _al.memorylocations and \
                _al.memorylocations[0].name == "x_sb":
            SB_BASE = _al.memorylocations[0].addr
    assert SB_BASE is not None
    xnb_off = SB_BASE + (_pc[0] + 31) // 32 * 32
    xnb = A("xnb", [128, 16, T], BF16)
    scrA = A("scrA", [128, TS], F32)
    scrB = A("scrB", [128, TS], F32)
    scrC = A("scrC", [128, T], BF16)
    p1s = A("p1s_t", [128, 128], F32)
    nslp = A("nslp_t", [128, 16], F32)
    dkbt = A("dkb_t", [128, 128], F32)
    difft = A("diff_t", [128, 128], F32)
    caust = A("caus_t", [128, 128], F32)
    h128t = A("h128_t", [128, 128], BF16)
    i128t = A("i128_t", [128, 128], BF16)
    ctt = A("ct_t", [128, 128], F32)
    mask8t = A("mask8_t", [128, 8], F32)
    onesc = A("onesc", [128, 1], BF16)
    onesr = A("onesr", [1, 128], BF16)
    onesrF = A("onesrF", [1, 128], F32)
    bqkvt = A("bqkv_t", [128, 48], F32)
    boutt = A("bout_t", [128, 16], F32)
    asbt = A("a_t", [128, 16 * NS], F32)
    bdtt = A("bdt_t", [128, 16], F32)
    dvect = A("dvec_t", [128, 16], F32)
    suugt = A("suug_t", [128, 16], F32)
    sudpt = A("sudp_t", [128, 64], F32)
    wscut = A("wscu_t", [128, 64], F32)
    wscgt = A("wscg_t", [128, 64], F32)
    wscdpt = A("wscdp_t", [128, 16], F32)
    lamct = A("lamc_t", [128, 16], F32)
    wst = A("wst", [128, 16, NS + 1], F32)
    est = A("est", [128, 16 * NS], F32)
    stats_off = (_pc[0] + 31) // 32 * 32
    stats = A("stats", [33, T], F32)
    stMR_off = (_pc[0] + 31) // 32 * 32
    stM = A("stM", [1, T], BF16)
    stR = A("stR", [1, T], BF16)
    rsq = A("rsq", [128, 2], F32)
    kb.init_consts([1.0, -1.0, 0.0, 0.5, 1.5, 2048.0, 1e-5], alloc=A)

    AR_BASE = SB_BASE + (_pc[0] + 31) // 32 * 32
    AR_SIZE = (SB_BASE + 212863 - 64) - AR_BASE
    _cur = [0]

    def ar_reset():
        _cur[0] = 0

    def ar(name, shape, dtype, at=None):
        nbytes = int(np.prod(shape[1:])) * (2 if dtype == BF16 else 4)
        if at is None:
            off = _cur[0]
            _cur[0] = (off + nbytes + 31) // 32 * 32
            assert _cur[0] <= AR_SIZE, f"arena overflow: {name} needs {_cur[0]} > {AR_SIZE}"
            return nc.alloc_sbuf_tensor_at(name, shape, dtype, offset=AR_BASE + off)
        return nc.alloc_sbuf_tensor_at(name, shape, dtype, offset=at)

    P = nc.alloc_psum_tensor
    pbA = P("pbA", [128, 512], F32)
    pbB = P("pbB", [128, 512], F32)
    pbC = P("pbC", [128, 512], F32)
    pbig = P("pbig", [128, 1024], F32)
    ptr = P("ptr", [128, 1024], BF16)
    pbig2 = P("pbig2", [128, 1024], F32)

    sl_tab = kb.slot("tab", oneshot=True)
    sl_tab2 = kb.slot("tab2", oneshot=True)
    sl_x = kb.slot("xl", oneshot=True)
    sl_out = kb.slot("outp", oneshot=True)
    sl_cb = kb.slot("cb")
    sl_cg = kb.slot("cg")

    for dst, src in ((nslp, dnslp), (p1s, dp1), (dkbt, ddk), (difft, ddiff),
                     (caust, dcaus), (ctt, dct), (mask8t, dmask8), (bqkvt, dbqkv),
                     (boutt, dbout), (bdtt, dbdt), (dvect, ddvec), (suugt, dsuug),
                     (sudpt, dsudp), (wscut, dwscu), (wscgt, dwscg), (wscdpt, dwscdp)):
        E("SPD", [], [dst.name], lambda e, d=dst, s=src: e.dma_start(out=d.ap(), in_=s.ap()),
          slot=sl_tab)
    E("SPD", [], ["a_t"], lambda e: e.dma_start(out=asbt.ap(), in_=dalog.ap()), slot=sl_tab)
    for dst, src in ((h128t, dh128), (i128t, di128)):
        E("POOLD", [], [dst.name], lambda e, d=dst, s=src: e.dma_start(out=d.ap(), in_=s.ap()),
          slot=sl_tab2)
    E("DVE", [], ["ones"], lambda e: (e.memset(onesc.ap(), 1.0), e.memset(onesr.ap(), 1.0),
                                      e.memset(onesrF.ap(), 1.0))[-1])
    E("ACT", ["a_t"], ["a_t"], lambda e: e.activation(asbt.ap(), asbt.ap(), AF.Exp))
    E("DVE", ["a_t"], ["a_t"], lambda e: e.tensor_scalar_mul(asbt.ap(), asbt.ap(), -1.0))
    E("ACT", ["nslp_t"], ["lamc_t"], lambda e: e.activation(
        lamct.ap(), nslp.ap(), AF.Exp, bias=0.0, scale=128.0))
    E("ACT", [], ["rsq"], lambda e: e.activation(rsq[:, 1:2], kb._consts[2048.0], AF.Sqrt))
    E("DVE", ["rsq"], ["rsq"], lambda e: e.reciprocal(rsq[:, 1:2], rsq[:, 1:2]))
    E("ACT", ["rsq"], ["rsq"], lambda e: e.activation(rsq[:, 0:1], kb._consts[0.5], AF.Sqrt))
    E("DVE", ["rsq", "wscu_t"], ["wscu_t"], lambda e: e.tensor_scalar(
        wscut.ap(), wscut.ap(), rsq[:, 1:2], None, alu.mult))
    E("DVE", ["rsq", "wscg_t"], ["wscg_t"], lambda e: e.tensor_scalar(
        wscgt.ap(), wscgt.ap(), rsq[:, 1:2], None, alu.mult))
    E("DVE", ["rsq", "wscdp_t"], ["wscdp_t"], lambda e: e.tensor_scalar(
        wscdpt.ap(), wscdpt.ap(), rsq[:, 0:1], 1.0 / 64.0, alu.mult, alu.mult))
    E("SPD", [], ["x_load"], lambda e: e.dma_start(
        out=x_sb.ap(), in_=dx.rearrange("(c p) t -> p c t", p=128)), slot=sl_x)
    for c in range(16):
        kb.reg["x%d" % c] = dict(kb.reg["x_load"])
    sl_pcg = kb.slot("pcg", oneshot=True)
    sl_pcu = kb.slot("pcu", oneshot=True)
    sl_pcd = kb.slot("pcd", oneshot=True)
    for i in range(8):
        E("POOLD", ["x_load"], ["dw1gb"], lambda e, i=i: e.dma_start(
            out=dw1gb[i * 256:(i + 1) * 256, :], in_=dw1g[i * 256:(i + 1) * 256, :]),
          slot=sl_pcg)
        E("POOLD", ["x_load"], ["dw1ub"], lambda e, i=i: e.dma_start(
            out=dw1ub[i * 256:(i + 1) * 256, :], in_=dw1u[i * 256:(i + 1) * 256, :]),
          slot=sl_pcu)
        E("POOLD", ["x_load"], ["dw1db"], lambda e, i=i: e.dma_start(
            out=dw1db[i * 1024:(i + 1) * 1024, :], in_=dw1d[i * 1024:(i + 1) * 1024, :]),
          slot=sl_pcd)

    def layernorm():
        for c in range(16):
            E("DVE", ["x%d" % c], ["scrC"], lambda e, c=c: e.tensor_copy(scrC[:, 0:T], x_sb[:, c, :]))
            for t in range(NT):
                E("PE", ["scrC", "ones"], ["pbig"], lambda e, c=c, t=t: e.matmul(
                    pbig[0:1, t * TS:(t + 1) * TS], onesc.ap(), scrC[:, t * TS:(t + 1) * TS],
                    start=(c == 0), stop=(c == 15)))
        E("DVE", ["pbig"], ["st0"], lambda e: e.tensor_scalar_mul(stats[32:33, 0:T], pbig[0:1, 0:T], 1.0 / DM))
        for c in range(16):
            E("ACT", ["x%d" % c], ["scrC"], lambda e, c=c: e.activation(
                scrC[:, 0:T], x_sb[:, c, :], AF.Square))
            for t in range(NT):
                E("PE", ["scrC", "ones", "st0"], ["pbig"], lambda e, c=c, t=t: e.matmul(
                    pbig[0:1, t * TS:(t + 1) * TS], onesc.ap(), scrC[:, t * TS:(t + 1) * TS],
                    start=(c == 0), stop=(c == 15)))
        E("DVE", ["pbig", "st0"], ["st1"], lambda e: e.tensor_scalar_mul(stats[0:1, 0:T], pbig[0:1, 0:T], 1.0 / DM))
        E("ACT", ["st0"], ["stM"], lambda e: e.activation(stM[0:1, 0:T], stats[32:33, 0:T], AF.Square))
        E("DVE", ["st1", "stM"], ["st1"], lambda e: e.tensor_tensor(
            stats[0:1, 0:T], stats[0:1, 0:T], stM[0:1, 0:T], alu.subtract))
        E("ACT", ["st1"], ["st1"], lambda e: e.activation(stats[0:1, 0:T], stats[0:1, 0:T],
                                                          AF.Sqrt, bias=1e-5))
        E("DVE", ["st1"], ["st1"], lambda e: e.reciprocal(stats[0:1, 0:T], stats[0:1, 0:T]))
        E("DVE", ["st1"], ["stR"], lambda e: e.tensor_copy(stR[0:1, 0:T], stats[0:1, 0:T]))
        E("ACT", ["st0"], ["stM"], lambda e: e.activation(stM[0:1, 0:T], stats[32:33, 0:T], AF.Copy))
        E("DVE", ["st1", "stM"], ["stM"], lambda e: e.tensor_tensor(
            stM[0:1, 0:T], stM[0:1, 0:T], stats[0:1, 0:T], alu.mult))
        for t in range(NT):
            E("PE", ["stR", "ones"], ["pbA"], lambda e, t=t: e.matmul(
                pbA[:, 0:TS], onesr.ap(), stR[0:1, t * TS:(t + 1) * TS], start=True, stop=True))
            E("PE", ["stM", "ones"], ["pbB"], lambda e, t=t: e.matmul(
                pbB[:, 0:TS], onesr.ap(), stM[0:1, t * TS:(t + 1) * TS], start=True, stop=True))
            for c in range(16):
                E("DVE", ["x%d" % c, "pbA"], ["scrA"], lambda e, c=c, t=t: e.tensor_tensor(
                    scrA[:, 0:TS], x_sb[:, c, t * TS:(t + 1) * TS], pbA[:, 0:TS], alu.mult))
                E("DVE", ["scrA", "pbB"], ["xnb%d" % c], lambda e, c=c, t=t: e.tensor_tensor(
                    xnb[:, c, t * TS:(t + 1) * TS], scrA[:, 0:TS], pbB[:, 0:TS], alu.subtract))

    layernorm()
    if upto == "ln1":
        _finish(kb, E, x_sb, dy, sl_out, T)
        return nc, kb

    ar_reset()
    spark = ar("spark", [128, H, 128], F32)
    kpark = ar("kpark", [128, H, T], BF16)
    vpark = ar("vpark", [128, H, T], BF16)
    wk_off = _cur[0]
    wkbuf = ar("wkbuf", [128, 16, 128], BF16)
    wvbuf = ar("wvbuf", [128, 16, 128], BF16)
    wqbuf = ar("wqbuf", [128, 16, 128], BF16)
    dkh = ar("dkh", [128, 128], BF16)
    dmh = ar("dmh", [128, 128], BF16)
    dqh = ar("dqh", [128, 128], BF16)
    qth = ar("qth", [128, T], BF16)
    qpc = ar("qpc", [128, 128], BF16)
    ktk = [ar("ktk%d" % i, [128, 128], BF16) for i in range(2)]
    kdec = [ar("kdec%d" % i, [128, 128], BF16) for i in range(2)]
    msc = [ar("msc%d" % i, [128, 128], BF16) for i in range(2)]
    sbf = ar("sbf", [128, 128], BF16)
    walk = ar("walk", [128, H, 128], F32, at=AR_BASE + wk_off)
    sl_wk = kb.slot("wk")
    sl_wv = kb.slot("wv")
    sl_wq = kb.slot("wq")
    sl_wo = kb.slot("wo")

    def head_w(e, dst, col):
        return e.dma_start(out=dst.ap(), in_=dwqkv[:, col:col + HD]
                           .rearrange("(kc p) f -> p kc f", p=128))

    for h in range(H):
        E("POOLD", [], ["wkbuf"], lambda e, h=h: head_w(e, wkbuf, DM + h * HD), slot=sl_wk)
        E("POOLD", [], ["wvbuf"], lambda e, h=h: head_w(e, wvbuf, 2 * DM + h * HD), slot=sl_wv)
        for t in range(NT):
            E("PE", ["wkbuf"] + ["xnb%d" % c for c in range(16)], ["pbig"],
              lambda e, t=t: [e.matmul(pbig[:, t * TS:(t + 1) * TS], wkbuf[:, kc, :],
                                       xnb[:, kc, t * TS:(t + 1) * TS],
                                       start=(kc == 0), stop=(kc == 15)) for kc in range(16)][-1])
            E("ACT", ["pbig", "bqkv_t"], ["kp%d" % h], lambda e, h=h, t=t: e.activation(
                kpark[:, h, t * TS:(t + 1) * TS], pbig[:, t * TS:(t + 1) * TS], AF.Identity,
                bias=bqkvt[:, 16 + h:17 + h]))
        for t in range(NT):
            E("PE", ["wvbuf"] + ["xnb%d" % c for c in range(16)], ["pbig"],
              lambda e, t=t: [e.matmul(pbig[:, t * TS:(t + 1) * TS], wvbuf[:, kc, :],
                                       xnb[:, kc, t * TS:(t + 1) * TS],
                                       start=(kc == 0), stop=(kc == 15)) for kc in range(16)][-1])
            E("ACT", ["pbig", "bqkv_t"], ["scrC"], lambda e, h=h, t=t: e.activation(
                scrC[:, t * TS:(t + 1) * TS], pbig[:, t * TS:(t + 1) * TS], AF.Identity,
                bias=bqkvt[:, 32 + h:33 + h]))
        for cc in range(NCH):
            E("PE", ["scrC", "i128_t"], ["ptr"], lambda e, cc=cc: e.transpose(
                ptr[:, cc * 128:cc * 128 + 128], scrC[:, cc * 128:(cc + 1) * 128], i128t.ap()))
            E("DVE", ["ptr"], ["vp%d" % h], lambda e, h=h, cc=cc: e.tensor_copy(
                vpark[:, h, cc * 128:(cc + 1) * 128], ptr[:, cc * 128:cc * 128 + 128]))
        E("ACT", ["dkb_t", "nslp_t"], ["dkh"], lambda e, h=h: e.activation(
            dkh.ap(), dkbt.ap(), AF.Exp, bias=0.0, scale=nslp[:, h:h + 1]))
        E("DVE", [], ["spk%d" % h], lambda e, h=h: e.memset(spark[:, h, :], 0.0))
        for cc in range(NCH):
            csl = slice(cc * 128, (cc + 1) * 128)
            pp = cc % 2
            kd, kdk = kdec[pp], "kdec%d" % pp
            kt, ktkk = ktk[pp], "ktk%d" % pp
            upb, upk = (pbA, "pbA") if pp == 0 else (pbC, "pbC")
            psl = slice(pp * 128, pp * 128 + 128)
            E("DVE", ["kp%d" % h, "dkh"], [kdk], lambda e, h=h, csl=csl, kd=kd: e.tensor_tensor(
                kd.ap(), kpark[:, h, csl], dkh.ap(), alu.mult))
            E("PE", [kdk, "i128_t"], ["ptr"], lambda e, kd=kd, psl=psl: e.transpose(
                ptr[:, psl], kd.ap(), i128t.ap()))
            E("DVE", ["ptr"], [ktkk], lambda e, kt=kt, psl=psl: e.tensor_copy(kt.ap(), ptr[:, psl]))
            E("PE", [ktkk, "vp%d" % h], [upk], lambda e, h=h, csl=csl, kt=kt, upb=upb: e.matmul(
                upb[:, 0:128], kt.ap(), vpark[:, h, csl], start=True, stop=True))
            E("DVE", [upk, "lamc_t", "spk%d" % h], ["spk%d" % h],
              lambda e, h=h, upb=upb: e.scalar_tensor_tensor(
                  spark[:, h, :], spark[:, h, :], lamct[:, h:h + 1], upb[:, 0:128],
                  alu.mult, alu.add))

    E("SPD", ["spk%d" % h for h in range(H)], ["wloc0"], lambda e: e.dma_start(
        out=dwloc.rearrange("(h p) d -> p h d", p=128), in_=spark.ap()), slot=sl_cb)
    E("POOL", ["wloc0"], ["wall0"], lambda e: e.collective_compute(
        "AllGather", mybir.AluOpType.bypass, replica_groups=[list(range(NCORES))],
        ins=[dwloc.ap()], outs=[dwall.ap()]))
    for h in range(H):
        E("DVE", [], ["spk%d" % h], lambda e, h=h: e.memset(spark[:, h, :], 0.0))
    for k in range(NCORES):
        E("SPD", ["wall0"] + (["wkbuf", "wvbuf"] if k == 0 else []), ["walk"],
          lambda e, k=k: e.dma_start(
              out=walk.ap(), in_=dwall[k * DM:(k + 1) * DM, :]
              .rearrange("(h p) d -> p h d", p=128)), slot=sl_cg)
        for h in range(H):
            E("DVE", ["walk", "ct_t", "spk%d" % h], ["spk%d" % h],
              lambda e, h=h, k=k: e.scalar_tensor_tensor(
                  spark[:, h, :], walk[:, h, :], ctt[:, h * 8 + k:h * 8 + k + 1],
                  spark[:, h, :], alu.mult, alu.add))

    for h in range(H):
        E("POOLD", [], ["wqbuf"], lambda e, h=h: head_w(e, wqbuf, h * HD), slot=sl_wq)
        E("ACT", ["p1s_t", "nslp_t"], ["dqh"], lambda e, h=h: e.activation(
            dqh.ap(), p1s.ap(), AF.Exp, bias=0.0, scale=nslp[:, h:h + 1]))
        E("ACT", ["diff_t", "nslp_t"], ["dmh"], lambda e, h=h: e.activation(
            dmh.ap(), difft.ap(), AF.Exp, bias=0.0, scale=nslp[:, h:h + 1]))
        E("DVE", ["dmh", "caus_t"], ["dmh"], lambda e: e.tensor_tensor(
            dmh.ap(), dmh.ap(), caust.ap(), alu.mult))
        E("ACT", ["dkb_t", "nslp_t"], ["dkh"], lambda e, h=h: e.activation(
            dkh.ap(), dkbt.ap(), AF.Exp, bias=0.0, scale=nslp[:, h:h + 1]))
        for t in range(NT):
            E("PE", ["wqbuf"] + ["xnb%d" % c for c in range(16)], ["pbB"],
              lambda e, t=t: [e.matmul(pbB[:, 0:TS], wqbuf[:, kc, :],
                                       xnb[:, kc, t * TS:(t + 1) * TS],
                                       start=(kc == 0), stop=(kc == 15)) for kc in range(16)][-1])
            E("ACT", ["pbB", "bqkv_t"], ["qth"], lambda e, h=h, t=t: e.activation(
                qth[:, t * TS:(t + 1) * TS], pbB[:, 0:TS], AF.Identity, bias=bqkvt[:, h:h + 1]))
        for cc in range(NCH):
            csl = slice(cc * 128, (cc + 1) * 128)
            pp = cc % 2
            kd, kdk = kdec[pp], "kdec%d" % pp
            kt, ktkk = ktk[pp], "ktk%d" % pp
            ms, msck = msc[pp], "msc%d" % pp
            spb, spk_ = (pbA, "pbA") if pp == 0 else (pbC, "pbC")
            psl = slice(pp * 128, pp * 128 + 128)
            E("PE", ["kp%d" % h, "qth"], [spk_], lambda e, h=h, csl=csl, spb=spb: e.matmul(
                spb[:, 0:128], kpark[:, h, csl], qth[:, csl], start=True, stop=True))
            E("DVE", [spk_, "dmh"], [msck], lambda e, spb=spb, ms=ms: e.tensor_tensor(
                ms.ap(), spb[:, 0:128], dmh.ap(), alu.mult))
            E("DVE", ["spk%d" % h], ["sbf"], lambda e, h=h: e.tensor_copy(sbf.ap(), spark[:, h, :]))
            E("DVE", ["qth", "dqh"], ["qpc"], lambda e, h=h, csl=csl: e.tensor_tensor(
                qpc.ap(), qth[:, csl], dqh.ap(), alu.mult))
            E("PE", ["vp%d" % h, msck, "sbf", "qpc"], ["pbig"],
              lambda e, h=h, csl=csl, ms=ms: (
                  e.matmul(pbig[:, csl], vpark[:, h, csl], ms.ap(), start=True, stop=False),
                  e.matmul(pbig[:, csl], sbf.ap(), qpc.ap(), start=False, stop=True))[-1])
            E("DVE", ["kp%d" % h, "dkh"], [kdk], lambda e, h=h, csl=csl, kd=kd: e.tensor_tensor(
                kd.ap(), kpark[:, h, csl], dkh.ap(), alu.mult))
            E("PE", [kdk, "i128_t"], ["ptr"], lambda e, kd=kd, psl=psl: e.transpose(
                ptr[:, 256 + psl.start:256 + psl.stop], kd.ap(), i128t.ap()))
            E("DVE", ["ptr"], [ktkk], lambda e, kt=kt, psl=psl: e.tensor_copy(
                kt.ap(), ptr[:, 256 + psl.start:256 + psl.stop]))
            E("PE", [ktkk, "vp%d" % h], ["pbig2"], lambda e, h=h, csl=csl, kt=kt: e.matmul(
                pbig2[:, 0:128], kt.ap(), vpark[:, h, csl], start=True, stop=True))
            E("DVE", ["pbig2", "lamc_t", "spk%d" % h], ["spk%d" % h],
              lambda e, h=h: e.scalar_tensor_tensor(
                  spark[:, h, :], spark[:, h, :], lamct[:, h:h + 1], pbig2[:, 0:128],
                  alu.mult, alu.add))
        E("DVE", ["pbig"], ["vp%d" % h], lambda e, h=h: e.tensor_copy(
            vpark[:, h, 0:T], pbig[:, 0:T]))

    for f in range(16):
        E("POOLD", [], ["wqbuf"], lambda e, f=f: e.dma_start(
            out=wqbuf.ap(), in_=dwout[:, f * 128:(f + 1) * 128]
            .rearrange("(kc p) g -> p kc g", p=128)), slot=sl_wo)
        for t in range(NT):
            E("PE", ["wqbuf"] + ["vp%d" % hh for hh in range(H)], ["pbA"],
              lambda e, t=t: [e.matmul(pbA[:, 0:TS], wqbuf[:, kc, :],
                                       vpark[:, kc, t * TS:(t + 1) * TS],
                                       start=(kc == 0), stop=(kc == 15)) for kc in range(16)][-1])
            E("DVE", ["pbA", "bout_t", "x%d" % f], ["x%d" % f],
              lambda e, f=f, t=t: e.scalar_tensor_tensor(
                  x_sb[:, f, t * TS:(t + 1) * TS], pbA[:, 0:TS], boutt[:, f:f + 1],
                  x_sb[:, f, t * TS:(t + 1) * TS], alu.add, alu.add))

    if upto == "l0attn":
        _finish(kb, E, x_sb, dy, sl_out, T)
        return nc, kb

    kb.barrier()
    ar_reset()
    wgs = ar("wgs", [128, 16, 4, 128], BF16)
    wus = ar("wus", [128, 16, 4, 128], BF16)
    wds = ar("wds", [128, 4, 16, 128], BF16)
    hfs = [ar("hf%d" % r, [128, T], BF16) for r in range(4)]
    hms = [ar("hm%d" % r, [128, T], BF16) for r in range(4)]
    cmp6 = [ar("cmpt%d" % i, [128, T], BF16) for i in range(6)]
    wint = [ar("wint%d" % i, [128, T], BF16) for i in range(2)]
    sl_g = kb.slot("gs")
    sl_u = kb.slot("us")
    sl_d = kb.slot("ds")

    def sparsify():
        for r in range(4):
            E("ACT", ["hf%d" % r], ["hm%d" % r], lambda e, r=r: e.activation(
                hms[r][:, 0:T], hfs[r][:, 0:T], AF.Abs))
        pairs = [(0, 1), (0, 2), (0, 3), (1, 2), (1, 3), (2, 3)]
        for i, (p, q) in enumerate(pairs):
            eng = "DVE"
            E(eng, ["hm%d" % p, "hm%d" % q], ["cmp%d" % i],
              lambda e, p=p, q=q, i=i: e.tensor_tensor(
                  cmp6[i][:, 0:T], hms[p][:, 0:T], hms[q][:, 0:T], alu.is_ge))
        E("DVE", ["cmp0", "cmp1"], ["win0"], lambda e: e.tensor_tensor(
            wint[0][:, 0:T], cmp6[0][:, 0:T], cmp6[1][:, 0:T], alu.add))
        E("DVE", ["win0", "cmp2"], ["win0"], lambda e: e.tensor_tensor(
            wint[0][:, 0:T], wint[0][:, 0:T], cmp6[2][:, 0:T], alu.add))
        E("DVE", ["win0"], ["win0"], lambda e: e.tensor_single_scalar(
            wint[0][:, 0:T], wint[0][:, 0:T], 1.5, alu.is_ge))
        E("DVE", ["win0", "hf0", "hm0"], ["hm0"], lambda e: e.tensor_tensor(
            hms[0][:, 0:T], hfs[0][:, 0:T], wint[0][:, 0:T], alu.mult))
        E("POOL", ["cmp3", "cmp4"], ["win1"], lambda e: e.tensor_tensor(
            wint[1][:, 0:T], cmp6[3][:, 0:T], cmp6[4][:, 0:T], alu.add))
        E("POOL", ["win1", "cmp0"], ["win1"], lambda e: e.tensor_tensor(
            wint[1][:, 0:T], wint[1][:, 0:T], cmp6[0][:, 0:T], alu.subtract))
        E("DVE", ["win1"], ["win1"], lambda e: e.tensor_single_scalar(
            wint[1][:, 0:T], wint[1][:, 0:T], 0.5, alu.is_ge))
        E("POOL", ["win1", "hf1", "hm1"], ["hm1"], lambda e: e.tensor_tensor(
            hms[1][:, 0:T], hfs[1][:, 0:T], wint[1][:, 0:T], alu.mult))
        E("DVE", ["cmp5", "cmp1", "win0"], ["win0"], lambda e: e.tensor_tensor(
            wint[0][:, 0:T], cmp6[5][:, 0:T], cmp6[1][:, 0:T], alu.subtract))
        E("DVE", ["win0", "cmp3"], ["win0"], lambda e: e.tensor_tensor(
            wint[0][:, 0:T], wint[0][:, 0:T], cmp6[3][:, 0:T], alu.subtract))
        E("DVE", ["win0"], ["win0"], lambda e: e.tensor_single_scalar(
            wint[0][:, 0:T], wint[0][:, 0:T], -0.5, alu.is_ge))
        E("DVE", ["win0", "hf2", "hm2"], ["hm2"], lambda e: e.tensor_tensor(
            hms[2][:, 0:T], hfs[2][:, 0:T], wint[0][:, 0:T], alu.mult))
        E("POOL", ["cmp2", "cmp4", "win1"], ["win1"], lambda e: e.tensor_tensor(
            wint[1][:, 0:T], cmp6[2][:, 0:T], cmp6[4][:, 0:T], alu.add))
        E("POOL", ["win1", "cmp5"], ["win1"], lambda e: e.tensor_tensor(
            wint[1][:, 0:T], wint[1][:, 0:T], cmp6[5][:, 0:T], alu.add))
        E("DVE", ["win1"], ["win1"], lambda e: e.tensor_single_scalar(
            wint[1][:, 0:T], wint[1][:, 0:T], 1.5, alu.is_le))
        E("POOL", ["win1", "hf3", "hm3"], ["hm3"], lambda e: e.tensor_tensor(
            hms[3][:, 0:T], hfs[3][:, 0:T], wint[1][:, 0:T], alu.mult))

    for s in range(IN0 // 512):
        E("POOLD", [], ["wgs"], lambda e, s=s: e.dma_start(
            out=wgs.ap(), in_=dwg[:, s * 512:(s + 1) * 512]
            .rearrange("(kc p) (r g) -> p kc r g", p=128, g=128)), slot=sl_g)
        E("POOLD", [], ["wus"], lambda e, s=s: e.dma_start(
            out=wus.ap(), in_=dwu[:, s * 512:(s + 1) * 512]
            .rearrange("(kc p) (r g) -> p kc r g", p=128, g=128)), slot=sl_u)
        E("POOLD", [], ["wds"], lambda e, s=s: e.dma_start(
            out=wds.ap(), in_=dwd[s * 512:(s + 1) * 512, :]
            .rearrange("(r p) (fc g) -> p r fc g", p=128, g=128)), slot=sl_d)
        for r in range(4):
            for t in range(NT):
                tsl = slice(t * TS, (t + 1) * TS)
                pp = (r * NT + t) % 2
                gpb, gpk = (pbA, "pbA") if pp == 0 else (pbC, "pbC")
                upb, upk = (pbB, "pbB") if pp == 0 else (pbig2, "pbig2")
                E("PE", ["wgs"] + ["xnb%d" % c for c in range(16)], [gpk],
                  lambda e, r=r, tsl=tsl, gpb=gpb: [e.matmul(gpb[:, 0:TS], wgs[:, kc, r, :],
                                                             xnb[:, kc, tsl], start=(kc == 0),
                                                             stop=(kc == 15)) for kc in range(16)][-1])
                E("PE", ["wus"] + ["xnb%d" % c for c in range(16)], [upk],
                  lambda e, r=r, tsl=tsl, upb=upb: [e.matmul(upb[:, 0:TS], wus[:, kc, r, :],
                                                             xnb[:, kc, tsl], start=(kc == 0),
                                                             stop=(kc == 15)) for kc in range(16)][-1])
                E("ACT", [gpk], ["scrA"], lambda e, gpb=gpb: e.activation(
                    scrA[:, 0:TS], gpb[:, 0:TS], AF.Silu))
                E("DVE", ["scrA", upk], ["hf%d" % r], lambda e, r=r, tsl=tsl, upb=upb:
                  e.tensor_tensor(hfs[r][:, tsl], scrA[:, 0:TS], upb[:, 0:TS], alu.mult))
        sparsify()
        for f in range(16):
            for t in range(NT):
                tsl = slice(t * TS, (t + 1) * TS)
                pb, pbk = (pbC, "pbC") if (f * NT + t) % 2 == 0 else (pbig, "pbig")
                E("PE", ["wds", "hm0", "hm1", "hm2", "hm3"], [pbk],
                  lambda e, f=f, tsl=tsl, pb=pb: [e.matmul(pb[:, 0:TS], wds[:, r, f, :],
                                                           hms[r][:, tsl], start=(r == 0),
                                                           stop=(r == 3)) for r in range(4)][-1])
                E("DVE", [pbk, "x%d" % f], ["x%d" % f], lambda e, f=f, tsl=tsl, pb=pb:
                  e.tensor_tensor(x_sb[:, f, tsl], x_sb[:, f, tsl], pb[:, 0:TS], alu.add))

    if upto == "l0":
        _finish(kb, E, x_sb, dy, sl_out, T)
        return nc, kb

    layernorm()

    kb.barrier()
    ar_reset()
    wdtb_off = _cur[0]
    wdtb = ar("wdtb", [128, 16, RK], BF16)
    wdob = ar("wdob", [RK, 16, 128], BF16)
    wbcb = ar("wbcb", [128, 16, 2 * NS], BF16)
    dtrt = ar("dtrt", [RK, T], BF16)
    bct_off = _cur[0]
    bct_f = ar("bct_f", [2 * NS, T], BF16)
    bpark = ar("bpark", [128, NS, T], BF16)
    cpark = ar("cpark", [128, NS, T], BF16)
    dtT = ar("dtT", [128, T], BF16)
    dtu = ar("dtu", [128, T], BF16)
    bdb = ar("bdb", [128, 2, T], BF16)
    xsc = ar("xsc", [128, 2, T], BF16)
    ssc = ar("ssc", [128, T], BF16)
    ssc2 = ar("ssc2", [128, T], BF16, at=AR_BASE + wdtb_off)
    ytl = ar("ytl", [128, T], BF16, at=AR_BASE + bct_off)
    dtTB = ar("dtTB", [128, T], BF16, at=SB_BASE + stats_off)
    sdtf = ar("sdtf", [128, 16], F32)
    if 1600 + 2 * T <= 4 * T:
        walS = ar("walS", [128, 16, NS + 1], BF16, at=SB_BASE + stMR_off)
        pkt = ar("pkt", [128, 16 * NS], F32, at=SB_BASE + stMR_off + 576)
        dtuB = ar("dtuB", [128, T], BF16, at=SB_BASE + stMR_off + 1600)
    else:
        walS = ar("walS", [128, 16, NS + 1], BF16)
        pkt = ar("pkt", [128, 16 * NS], F32)
        dtuB = ar("dtuB", [128, T], BF16)
    sl_dt = kb.slot("dtw", oneshot=True)
    sl_cb2 = kb.slot("cb2")
    sl_cg2 = kb.slot("cg2")

    E("POOLD", [], ["wdtb"], lambda e: e.dma_start(
        out=wdtb.ap(), in_=dwdtin.rearrange("(kc p) r -> p kc r", p=128)), slot=sl_dt)
    E("POOLD", [], ["wbcb"], lambda e: e.dma_start(
        out=wbcb.ap()[:, :, 0:NS], in_=dwb.rearrange("(kc p) n -> p kc n", p=128)), slot=sl_dt)
    E("POOLD", [], ["wbcb"], lambda e: e.dma_start(
        out=wbcb.ap()[:, :, NS:2 * NS], in_=dwc.rearrange("(kc p) n -> p kc n", p=128)), slot=sl_dt)
    E("POOLD", [], ["wdob"], lambda e: e.dma_start(
        out=wdob.ap(), in_=dwdtout.rearrange("k (fc p) -> k fc p", p=128)), slot=sl_dt)

    for t in range(NT):
        E("PE", ["wdtb"] + ["xnb%d" % c for c in range(16)], ["pbig"],
          lambda e, t=t: [e.matmul(pbig[0:RK, t * TS:(t + 1) * TS], wdtb[:, kc, :],
                                   xnb[:, kc, t * TS:(t + 1) * TS], start=(kc == 0),
                                   stop=(kc == 15)) for kc in range(16)][-1])
    E("DVE", ["pbig"], ["dtrt"], lambda e: e.tensor_copy(dtrt[:, 0:T], pbig[0:RK, 0:T]))
    for t in range(NT):
        E("PE", ["wbcb"] + ["xnb%d" % c for c in range(16)], ["pbA"],
          lambda e, t=t: [e.matmul(pbA[0:2 * NS, 0:TS], wbcb[:, kc, :],
                                   xnb[:, kc, t * TS:(t + 1) * TS], start=(kc == 0),
                                   stop=(kc == 15)) for kc in range(16)][-1])
        E("DVE", ["pbA"], ["bct_f"], lambda e, t=t: e.tensor_copy(
            bct_f[:, t * TS:(t + 1) * TS], pbA[0:2 * NS, 0:TS]))

    sl_bc = kb.slot("bcst", oneshot=True)
    E("POOLD", ["bct_f"], ["dbc"], lambda e: e.dma_start(
        out=dbc[:, 0:T], in_=bct_f.ap()), slot=sl_bc)

    def bcast_park(park, parkkey, row0):
        for n in range(NS):
            E("POOLD", ["dbc"], [parkkey], lambda e, n=n: e.dma_start(
                out=park[:, n, :], in_=dbc[row0 + n:row0 + n + 1, 0:T]
                .partition_broadcast(128)), slot=sl_bc)

    bcast_park(bpark, "bpark", 0)
    bcast_park(cpark, "cpark", NS)
    if upto == "ssmproj":
        _finish(kb, E, x_sb, dy, sl_out, T)
        return nc, kb

    dtTs, dtus = [dtT, dtTB], [dtu, dtuB]

    def dt_chunk(hc):
        p = hc % 2
        dT, dU, kT, kU = dtTs[p], dtus[p], "dtT%d" % p, "dtu%d" % p
        for t in range(NT):
            tsl = slice(t * TS, (t + 1) * TS)
            E("PE", ["dtrt", "wdob"], ["pbB"], lambda e, hc=hc, t=t: e.matmul(
                pbB[:, 0:TS], wdob[:, hc, :], dtrt[:, t * TS:(t + 1) * TS],
                start=True, stop=True))
            E("ACT", ["pbB", "bdt_t"], ["scrA"], lambda e, hc=hc: e.activation(
                scrA[:, 0:TS], pbB[:, 0:TS], AF.Abs, bias=bdtt[:, hc:hc + 1]))
            E("ACT", ["scrA"], ["scrA"], lambda e: e.activation(
                scrA[:, 0:TS], scrA[:, 0:TS], AF.Exp, bias=0.0, scale=-1.0))
            E("ACT", ["scrA"], ["scrA"], lambda e: e.activation(
                scrA[:, 0:TS], scrA[:, 0:TS], AF.Ln, bias=1.0))
            E("DVE", ["pbB", "bdt_t"], ["scrB"], lambda e, hc=hc: e.tensor_scalar(
                scrB[:, 0:TS], pbB[:, 0:TS], bdtt[:, hc:hc + 1], 0.0, alu.add, alu.max))
            E("DVE", ["scrA", "scrB"], [kT], lambda e, tsl=tsl, dT=dT: e.tensor_tensor(
                dT[:, tsl], scrA[:, 0:TS], scrB[:, 0:TS], alu.add))
            E("DVE", [kT, "xnb%d" % hc], [kU], lambda e, hc=hc, tsl=tsl, dT=dT, dU=dU:
              e.tensor_tensor(dU[:, tsl], dT[:, tsl], xnb[:, hc, tsl], alu.mult))

    if upto != "ssmdt":
        dt_chunk(0)
    for hc in (range(16) if upto != "ssmdt" else []):
        p = hc % 2
        dT, kT, kU = dtTs[p], "dtT%d" % p, "dtu%d" % p
        E("DVE", [kT], ["wst"], lambda e, hc=hc, dT=dT: e.reduce_sum(
            wst[:, hc, NS:NS + 1], dT[:, 0:T], axis=mybir.AxisListType.X))
        for n in range(NS):
            par = n % 2
            adp = (pbig, "pbig") if par == 0 else (pbig2, "pbig2")
            E("POOL", [kU, "bpark"], ["bdb%d" % par], lambda e, n=n, par=par, p=p:
              e.tensor_tensor(bdb[:, par, :], dtus[p][:, 0:T], bpark[:, n, :], alu.mult))
            E("ACT", [kT, "a_t"], [adp[1]], lambda e, hc=hc, n=n, adp=adp, dT=dT: e.activation(
                adp[0][:, 0:T], dT[:, 0:T], AF.Exp, bias=0.0,
                scale=asbt[:, hc * NS + n:hc * NS + n + 1]))
            E("DVE", [adp[1], "bdb%d" % par], ["xsc%d" % par],
              lambda e, par=par, adp=adp: e.tensor_tensor_scan(
                  xsc[:, par, :], adp[0][:, 0:T], bdb[:, par, :], 0.0, alu.mult, alu.add))
            E("DVE", ["xsc%d" % par, "wst"], ["wst"], lambda e, hc=hc, n=n, par=par: e.tensor_copy(
                wst[:, hc, n:n + 1], xsc[:, par, T - 1:T]))
            if n == 2 and hc + 1 < 16:
                dt_chunk(hc + 1)
    if upto == "ssmdt":
        for hc in range(2):
            dt_chunk(hc)
        _finish(kb, E, x_sb, dy, sl_out, T)
        return nc, kb
    if upto == "ssmp1":
        _finish(kb, E, x_sb, dy, sl_out, T)
        return nc, kb
    E("SPD", ["wst"], ["sloc1"], lambda e: e.dma_start(
        out=dsloc.rearrange("(hc p) c -> p hc c", p=128), in_=wst.ap()), slot=sl_cb2)
    E("POOL", ["sloc1"], ["sall1"], lambda e: e.collective_compute(
        "AllGather", mybir.AluOpType.bypass, replica_groups=[list(range(NCORES))],
        ins=[dsloc.ap()], outs=[dsall.ap()]))
    E("DVE", [], ["est"], lambda e: e.memset(est.ap(), 0.0))
    for k in range(NCORES):
        E("POOLD", ["sall1"], ["walS"], lambda e, k=k: e.dma_start(
            out=walS.ap(), in_=dsall[k * DM:(k + 1) * DM, :]
            .rearrange("(hc p) c -> p hc c", p=128)), slot=sl_cg2)
        E("DVE", ["walS"], ["sdtf"], lambda e: e.tensor_copy(
            sdtf.ap(), walS[:, :, NS]))
        for hc in range(16):
            E("DVE", ["sdtf", "a_t", "mask8_t"], ["pkt"], lambda e, k=k, hc=hc:
              e.tensor_scalar(pkt[:, hc * NS:(hc + 1) * NS],
                              asbt[:, hc * NS:(hc + 1) * NS],
                              sdtf[:, hc:hc + 1], mask8t[:, k:k + 1],
                              alu.mult, alu.mult))
        E("ACT", ["pkt"], ["pkt"], lambda e: e.activation(pkt.ap(), pkt.ap(), AF.Exp))
        E("DVE", ["est", "pkt"], ["est"], lambda e: e.tensor_tensor(
            est.ap(), est.ap(), pkt.ap(), alu.mult))
        E("DVE", ["walS", "mask8_t", "est"], ["est"], lambda e, k=k: e.scalar_tensor_tensor(
            est.ap().rearrange("p (hc n) -> p hc n", n=NS), walS.ap()[:, :, 0:NS],
            mask8t[:, k:k + 1], est.ap().rearrange("p (hc n) -> p hc n", n=NS),
            alu.mult, alu.add))
    dt_chunk(0)
    for hc in range(16):
        p = hc % 2
        dT, kT, kU = dtTs[p], "dtT%d" % p, "dtu%d" % p
        pend = None
        for n in range(NS):
            par = n % 2
            adp = (pbig, "pbig") if par == 0 else (pbig2, "pbig2")
            E("POOL", [kU, "bpark"], ["bdb%d" % par], lambda e, n=n, par=par, p=p:
              e.tensor_tensor(bdb[:, par, :], dtus[p][:, 0:T], bpark[:, n, :], alu.mult))
            E("ACT", [kT, "a_t"], [adp[1]], lambda e, hc=hc, n=n, adp=adp, dT=dT: e.activation(
                adp[0][:, 0:T], dT[:, 0:T], AF.Exp, bias=0.0,
                scale=asbt[:, hc * NS + n:hc * NS + n + 1]))
            E("DVE", [adp[1], "bdb%d" % par, "est"], ["xsc%d" % par],
              lambda e, hc=hc, n=n, par=par, adp=adp: e.tensor_tensor_scan(
                  xsc[:, par, :], adp[0][:, 0:T], bdb[:, par, :],
                  est[:, hc * NS + n:hc * NS + n + 1], alu.mult, alu.add))
            if n == 0:
                E("DVE", ["xsc0", "cpark"], ["ytl"] + (["bct_f"] if hc == 0 else []),
                  lambda e: e.tensor_tensor(ytl[:, 0:T], xsc[:, 0, :], cpark[:, 0, :],
                                            alu.mult))
            else:
                sscb, ssck = (ssc, "ssc") if par == 0 else (ssc2, "ssc2")
                E("DVE", ["xsc%d" % par, "cpark"], [ssck], lambda e, n=n, par=par, sscb=sscb:
                  e.tensor_tensor(sscb[:, 0:T], xsc[:, par, :], cpark[:, n, :], alu.mult))
                if pend is not None:
                    sb_, sk_ = pend
                    E("POOL", [sk_, "ytl"], ["ytl"], lambda e, sb_=sb_: e.tensor_tensor(
                        ytl[:, 0:T], ytl[:, 0:T], sb_[:, 0:T], alu.add))
                pend = (sscb, ssck)
            if n == 2 and hc + 1 < 16:
                dt_chunk(hc + 1)
        if pend is not None:
            sb_, sk_ = pend
            E("POOL", [sk_, "ytl"], ["ytl"], lambda e, sb_=sb_: e.tensor_tensor(
                ytl[:, 0:T], ytl[:, 0:T], sb_[:, 0:T], alu.add))
        E("DVE", ["ytl", "dvec_t", "xnb%d" % hc], ["ytl"], lambda e, hc=hc: e.scalar_tensor_tensor(
            ytl[:, 0:T], xnb[:, hc, :], dvect[:, hc:hc + 1], ytl[:, 0:T], alu.mult, alu.add))
        E("DVE", ["ytl", "x%d" % hc], ["x%d" % hc], lambda e, hc=hc: e.tensor_tensor(
            x_sb[:, hc, :], x_sb[:, hc, :], ytl[:, 0:T], alu.add))

    if upto == "ssm":
        _finish(kb, E, x_sb, dy, sl_out, T)
        return nc, kb

    kb.barrier()
    ar_reset()
    suh_off = _cur[0]
    suh = ar("suh", [128, 16, 128], BF16)
    suhn_off = _cur[0]
    suhn = ar("suhn", [128, 16, 128], BF16)
    z2f = ar("z2f", [128, 16, T], BF16)
    midB = ar("midB", [128, 32, TH], BF16)
    w1gs_off = _cur[0]
    w1gs = ar("w1gs", [128, 16, 128], BF16)
    w1us = ar("w1us", [128, 16, 128], BF16)
    w1ds_off = _cur[0]
    w1ds = ar("w1ds", [128, 2, 16, 128], BF16)
    su2t = ar("su2t", [128, 128], BF16)
    su2tB = ar("su2tB", [128, 128], BF16)
    if 8 * TH <= 4 * T:
        btA = ar("btA", [128, 4, TH], BF16, at=SB_BASE + stMR_off)
    else:
        btA = ar("btA", [128, 4, TH], BF16)
    btB = ar("btB", [128, 4, TH], BF16, at=AR_BASE + suh_off)
    w1gsB = ar("w1gsB", [128, 16, 128], BF16, at=AR_BASE + suhn_off)
    w1usB = ar("w1usB", [128, 16, 128], BF16, at=AR_BASE + w1ds_off)
    w1dsB = ar("w1dsB", [128, 2, 16, 128], BF16, at=AR_BASE + w1gs_off)
    sl_g1 = kb.slot("g1s")
    sl_g1b = kb.slot("g1sb")
    sl_u1 = kb.slot("u1s")
    sl_u1b = kb.slot("u1sb")
    sl_d1 = kb.slot("d1s")
    sl_d1b = kb.slot("d1sb")
    H16M = _hadamard(16)

    for c in range(16):
        E("DVE", ["h128_t", "suug_t"], ["suh"], lambda e, c=c: e.tensor_scalar(
            suh[:, c, :], h128t.ap(), suugt[:, c:c + 1], None, alu.mult))
        E("DVE", ["suh"], ["suhn"], lambda e, c=c: e.tensor_scalar(
            suhn[:, c, :], suh[:, c, :], -1.0, None, alu.mult))
    for a2 in range(16):
        for t in range(NT):
            tg = slice(t * TS, (t + 1) * TS)
            E("PE", ["suh", "suhn"] + ["xnb%d" % c for c in range(16)], ["pbA"],
              lambda e, a2=a2, tg=tg: [e.matmul(
                  pbA[:, 0:TS], (suh if H16M[a2, q] > 0 else suhn)[:, q, :],
                  xnb[:, q, tg], start=(q == 0), stop=(q == 15))
                  for q in range(16)][-1])
            E("ACT", ["pbA"], ["z2f"], lambda e, a2=a2, tg=tg: e.activation(
                z2f[:, a2, tg], pbA[:, 0:TS], AF.Copy))
    kb.barrier()
    if NHALF >= 2:
        midA = ar("midA", [128, 32, TH], BF16, at=xnb_off)
    else:
        midA = ar("midA", [128, 32, TH], BF16)

    def midt(a):
        return midA[:, a, :] if a < 32 else midB[:, a - 32, :]

    for half in range(NHALF):
        hb = half * TH
        for a in range(64):
            pp = a % 2
            gb = w1gs if pp == 0 else w1gsB
            gk = "w1gs" if pp == 0 else "w1gsB"
            gsl = sl_g1 if pp == 0 else sl_g1b
            guok = ["gu_ok%d" % (half - 1)] if (half > 0 and a == 0) else []
            gwr = [gk] + (["suhn"] if (a == 1 and half == 0) else [])
            E("SPD", guok + ["dw1gb"], gwr, lambda e, a=a, gb=gb: e.dma_start(
                out=gb.ap(), in_=dw1gb[:, a * 128:(a + 1) * 128]
                .rearrange("(kc p) g -> p kc g", p=128)), slot=gsl)
            ub = w1us if pp == 0 else w1usB
            uk = "w1us" if pp == 0 else "w1usB"
            usl = sl_u1 if pp == 0 else sl_u1b
            E("SPD", guok + ["dw1ub"], [uk], lambda e, a=a, ub=ub: e.dma_start(
                out=ub.ap(), in_=dw1ub[:, a * 128:(a + 1) * 128]
                .rearrange("(kc p) g -> p kc g", p=128)), slot=usl)
            for t in range(NTH):
                tz = slice(hb + t * TSH, hb + (t + 1) * TSH)
                tloc = slice(t * TSH, (t + 1) * TSH)
                E("PE", [gk, "z2f"], ["pbB"],
                  lambda e, tz=tz, gb=gb: [e.matmul(pbB[:, 0:TSH], gb[:, kc, :],
                                                    z2f[:, kc, tz], start=(kc == 0),
                                                    stop=(kc == 15)) for kc in range(16)][-1])
                E("PE", [uk, "z2f"], ["pbC"],
                  lambda e, tz=tz, ub=ub: [e.matmul(pbC[:, 0:TSH], ub[:, kc, :],
                                                    z2f[:, kc, tz], start=(kc == 0),
                                                    stop=(kc == 15)) for kc in range(16)][-1])
                E("ACT", ["pbB", "wscg_t"], ["scrA"], lambda e, a=a: e.activation(
                    scrA[:, 0:TSH], pbB[:, 0:TSH], AF.Silu, bias=0.0,
                    scale=wscgt[:, a:a + 1]))
                E("DVE", ["pbC", "wscu_t"], ["scrB"], lambda e, a=a: e.tensor_scalar(
                    scrB[:, 0:TSH], pbC[:, 0:TSH], wscut[:, a:a + 1], None, alu.mult))
                E("DVE", ["scrA", "scrB"], ["mid%d" % a], lambda e, a=a, tloc=tloc:
                  e.tensor_tensor(midt(a)[:, tloc], scrA[:, 0:TSH], scrB[:, 0:TSH], alu.mult))
        for a in range(64):
            st2, sk2 = (su2t, "su2t") if a % 2 == 0 else (su2tB, "su2tB")
            ppb, ppk = (pbig, "pbig") if a % 2 == 0 else (pbig2, "pbig2")
            E("DVE", ["h128_t", "sudp_t"], [sk2], lambda e, a=a, st2=st2: e.tensor_scalar(
                st2.ap(), h128t.ap(), sudpt[:, a:a + 1], None, alu.mult))
            for t in range(NTH):
                tloc = slice(t * TSH, (t + 1) * TSH)
                E("PE", [sk2, "mid%d" % a], [ppk], lambda e, a=a, tloc=tloc, st2=st2, ppb=ppb:
                  e.matmul(ppb[:, 0:TSH], st2.ap(), midt(a)[:, tloc], start=True, stop=True))
                E("ACT", [ppk], ["mid%d" % a], lambda e, a=a, tloc=tloc, ppb=ppb: e.activation(
                    midt(a)[:, tloc], ppb[:, 0:TSH], AF.Copy))
        for st in range(3):
            stp = 4 ** st
            gord = [0, 4, 8, 12, 1, 5, 9, 13, 2, 6, 10, 14, 3, 7, 11, 15] \
                if st < 2 else list(range(16))
            for gi, g in enumerate(gord):
                lo = g % stp
                hi = g // stp
                base = hi * stp * 4 + lo
                ia, ib, ic, id_ = base, base + stp, base + 2 * stp, base + 3 * stp
                bt = btA if gi % 2 == 0 else btB
                bk = "btA%d" if gi % 2 == 0 else "btB%d"
                ex = ["suh"] if (half == 0 and st == 0 and gi == 1) else []
                tspec = [("DVE", 0, ia, ib, alu.add), ("POOL", 1, ia, ib, alu.subtract),
                         ("DVE", 2, ic, id_, alu.add), ("POOL", 3, ic, id_, alu.subtract)]
                for eng, ti, p, q, op in tspec:
                    E(eng, ["mid%d" % p, "mid%d" % q], [bk % ti] + ex,
                      lambda e, bt=bt, ti=ti, p=p, q=q, op=op: e.tensor_tensor(
                          bt[:, ti, 0:TH], midt(p)[:, 0:TH], midt(q)[:, 0:TH], op))
                    ex = []
                ospec = [("DVE", ia, 0, 2, alu.add), ("POOL", ib, 1, 3, alu.add),
                         ("DVE", ic, 0, 2, alu.subtract), ("DVE", id_, 1, 3, alu.subtract)]
                for eng, o, t1, t2, op in ospec:
                    E(eng, [bk % t1, bk % t2], ["mid%d" % o],
                      lambda e, bt=bt, o=o, t1=t1, t2=t2, op=op: e.tensor_tensor(
                          midt(o)[:, 0:TH], bt[:, t1, 0:TH], bt[:, t2, 0:TH], op))
        E("DVE", [], ["dn_ok%d" % half, "w1gs", "w1us", "w1usB"],
          lambda e: e.engine_nop())
        for s in range(32):
            wb, wk, dsl = (w1ds, "w1ds", sl_d1) if s % 2 == 0 else (w1dsB, "w1dsB", sl_d1b)
            E("SPD", ["dn_ok%d" % half, "dw1db"], [wk], lambda e, s=s, wb=wb: e.dma_start(
                out=wb.ap(), in_=dw1db[s * 256:(s + 1) * 256, :]
                .rearrange("(r p) (fc g) -> p r fc g", p=128, g=128)), slot=dsl)
            for f in range(16):
                for t in range(NTH):
                    tloc = slice(t * TSH, (t + 1) * TSH)
                    tg = slice(hb + t * TSH, hb + (t + 1) * TSH)
                    pb, pbk = [(pbC, "pbC"), (pbB, "pbB"), (pbA, "pbA"),
                               (pbig, "pbig")][(f + t) % 4]
                    E("PE", [wk, "mid%d" % (s * 2), "mid%d" % (s * 2 + 1)], [pbk],
                      lambda e, s=s, f=f, tloc=tloc, pb=pb, wb=wb: [e.matmul(
                          pb[:, 0:TSH], wb[:, r, f, :], midt(s * 2 + r)[:, tloc],
                          start=(r == 0), stop=(r == 1)) for r in range(2)][-1])
                    if (s + f) % 2 == 0:
                        E("DVE", [pbk, "wscdp_t", "x%d" % f], ["x%d" % f],
                          lambda e, f=f, tg=tg, pb=pb: e.scalar_tensor_tensor(
                              x_sb[:, f, tg], pb[:, 0:TSH], wscdpt[:, f:f + 1],
                              x_sb[:, f, tg], alu.mult, alu.add))
                    else:
                        sc, sck = (scrA, "scrA") if f % 2 == 0 else (scrB, "scrB")
                        E("ACT", [pbk, "wscdp_t"], [sck], lambda e, f=f, pb=pb, sc=sc:
                          e.activation(sc[:, 0:TSH], pb[:, 0:TSH], AF.Identity,
                                       bias=0.0, scale=wscdpt[:, f:f + 1]))
                        E("POOL", [sck, "x%d" % f], ["x%d" % f],
                          lambda e, f=f, tg=tg, sc=sc: e.tensor_tensor(
                              x_sb[:, f, tg], x_sb[:, f, tg], sc[:, 0:TSH], alu.add))
        E("DVE", [], ["gu_ok%d" % half, "w1ds", "w1dsB"], lambda e: e.engine_nop())

    _finish(kb, E, x_sb, dy, sl_out, T)
    return nc, kb


def _finish(kb, E, x_sb, dy, sl_out, T):
    for c in range(16):
        E("SPD", ["x%d" % c], [], lambda e, c=c: e.dma_start(
            out=dy[c * 128:(c + 1) * 128, :], in_=x_sb[:, c, :]), slot=sl_out)
    kb.barrier()


def prep_shared(ii):
    f32 = np.float32
    C = np.ascontiguousarray
    d = {}
    d["wqkvT"] = C(np.asarray(ii["L0_attn_w_qkv"]).T.astype(f32))
    d["woutT"] = C(np.asarray(ii["L0_attn_w_out"]).T.astype(f32))
    d["bqkv"] = C(np.asarray(ii["L0_attn_b_qkv"]).reshape(48, 128).T.astype(f32))
    d["bout"] = C(np.asarray(ii["L0_attn_b_out"]).reshape(16, 128).T.astype(f32))
    perm = (4 * np.arange(2048)[None, :] + np.arange(4)[:, None]).ravel()
    d["wg_p"] = C(np.asarray(ii["L0_ffn_w_gate"])[:, perm].astype(f32))
    d["wu_p"] = C(np.asarray(ii["L0_ffn_w_up"])[:, perm].astype(f32))
    d["wd_p"] = C(np.asarray(ii["L0_ffn_w_down"])[perm, :].astype(f32))
    d["wdtinT"] = C(np.asarray(ii["L1_attn_W_dt_in"]).T.astype(f32))
    d["wdtoutT"] = C(np.asarray(ii["L1_attn_W_dt_out"]).T.astype(f32))
    d["wbT"] = C(np.asarray(ii["L1_attn_W_B"]).T.astype(f32))
    d["wcT"] = C(np.asarray(ii["L1_attn_W_C"]).T.astype(f32))
    d["alog_l"] = C(np.asarray(ii["L1_attn_A_log"]).reshape(16, 128, NS)
                    .transpose(1, 0, 2).reshape(128, 16 * NS).astype(f32))
    d["bdt"] = C(np.asarray(ii["L1_attn_b_dt_out"]).reshape(16, 128).T.astype(f32))
    d["dvec"] = C(np.asarray(ii["L1_attn_D"]).reshape(16, 128).T.astype(f32))
    d["su_ug"] = C(np.asarray(ii["L1_ffn_SU_ug"]).reshape(16, 128).T.astype(f32))
    d["su_dp"] = C(np.asarray(ii["L1_ffn_SU_dp"]).reshape(64, 128).T.astype(f32))
    wsc = np.asarray(ii["L1_ffn_Wscale_ug"])
    d["wscu"] = C(wsc[:IN1].reshape(64, 128).T.astype(f32))
    d["wscg"] = C(wsc[IN1:].reshape(64, 128).T.astype(f32))
    d["wscdp"] = C(np.asarray(ii["L1_ffn_Wscale_dp"]).reshape(16, 128).T.astype(f32))
    d["w1upT"] = C(np.asarray(ii["L1_ffn_W_up"]).T.astype(f32))
    d["w1gateT"] = C(np.asarray(ii["L1_ffn_W_gate"]).T.astype(f32))
    d["w1downT"] = C(np.asarray(ii["L1_ffn_W_down"]).T.astype(f32))
    slopes = _slopes32(H)
    d["nslopes_b"] = C(np.broadcast_to(-slopes[None, :], (128, 16)).astype(f32))
    pos = np.arange(128, dtype=np.float64)
    d["dkb"] = C(np.broadcast_to((127.0 - pos)[None, :], (128, 128)).astype(f32))
    diff = pos[None, :] - pos[:, None]
    d["diffc"] = C(np.maximum(diff, 0.0).astype(f32))
    d["causal"] = C((diff >= 0).astype(f32))
    d["h128"] = C(_hadamard(128).astype(f32))
    d["i128"] = C(np.eye(128, dtype=f32))
    return d


def prep_core(ii, core, T, shared):
    f32 = np.float32
    C = np.ascontiguousarray
    d = dict(shared)
    x = np.asarray(ii["L0_attn_x"])[0]
    d["xT"] = C(x[core * T:(core + 1) * T, :].T.astype(f32))
    tpos = np.arange(128, dtype=np.float64)
    d["p1b"] = C(np.broadcast_to((tpos + 1.0)[None, :], (128, 128)).astype(f32))
    slopes = _slopes32(H).astype(np.float64)
    ct = np.zeros((16, 8), dtype=np.float64)
    for h in range(H):
        lcc = math.exp(-slopes[h] * T)
        for k in range(core):
            ct[h, k] = lcc ** (core - 1 - k)
    d["ct_l0"] = C(np.broadcast_to(ct.reshape(1, 128), (128, 128)).astype(f32))
    m8 = np.zeros(8, dtype=f32)
    m8[:core] = 1.0
    d["mask8"] = C(np.broadcast_to(m8[None, :], (128, 8)).astype(f32))
    return d


def _slopes32(n_heads):
    start = np.float32(2.0) ** np.float32(-(2.0 ** -(math.log2(n_heads) - 3)))
    return np.array([start ** np.float32(i + 1) for i in range(n_heads)],
                    dtype=np.float32)



_CACHE = {}


def kernel(**inputs):
    from concourse.bass_utils import run_bass_kernel_spmd
    T = 1024
    if "nc" not in _CACHE:
        _CACHE["nc"] = build_nc(T, "full")
    nc, kb = _CACHE["nc"]
    shared = prep_shared(inputs)
    in_maps = [prep_core(inputs, c, T, shared) for c in range(NCORES)]
    res = run_bass_kernel_spmd(nc, in_maps, core_ids=list(range(NCORES)))
    out = np.concatenate(
        [np.ascontiguousarray(np.asarray(r["yT"]).T) for r in res.results], axis=0)
    return out[None].astype(np.float32)
